# version 01
# baseline (speedup 1.0000x reference)
import numpy as np
import ml_dtypes
from contextlib import ExitStack
import concourse.bass as bass
import concourse.mybir as mybir
from concourse.bass_utils import run_bass_kernel_spmd

F32 = mybir.dt.float32
BF16 = mybir.dt.bfloat16
AF = mybir.ActivationFunctionType
OP = mybir.AluOpType
AX = mybir.AxisListType
BF = ml_dtypes.bfloat16

NCORES = 8
B, S, D = 32, 2048, 1024
NSEQ = B // NCORES
FC = D // 128
HEADS = 16
EXP = 8
EDIM = 2048
LAT = 512
QB = 512
NQB = S // QB

ENGS = ("sync", "gpsimd", "tensor", "vector", "scalar")


class Prog:

    def __init__(self, nc, ctx):
        self.nc = nc
        self.streams = {e: [] for e in ENGS}
        self.sems = {}
        self.cnt = {}
        self.watermark = {e: {} for e in ENGS}
        for name in ("PE", "DV", "AC", "PL"):
            self.sems[name] = ctx.enter_context(nc.semaphore(name))
            self.cnt[name] = 0
        self.ndm = 14
        for i in range(self.ndm):
            self.sems[f"DM{i}"] = ctx.enter_context(nc.semaphore(f"DM{i}"))
            self.cnt[f"DM{i}"] = 0
        self.dma_rot = 0
        self.ngp = 40
        for i in range(self.ngp):
            self.sems[f"GP{i}"] = ctx.enter_context(nc.semaphore(f"GP{i}"))
            self.cnt[f"GP{i}"] = 0
        self.gp_next = 0

    def group(self, queue="sync"):
        if queue == "gpsimd":
            g = f"GP{self.gp_next}"
            self.gp_next += 1
            assert self.gp_next <= self.ngp, "out of gpsimd dma semaphores"
            return g
        g = f"DM{self.dma_rot % self.ndm}"
        self.dma_rot += 1
        return g

    OWN_SEM = {"vector": "DV", "scalar": "AC", "gpsimd": "PL"}

    def op(self, eng, fn, waits=(), inc=None, inc_amt=1):
        own = self.OWN_SEM.get(eng)
        waits = list(waits)
        if own is not None and self.cnt[own] > 0:
            waits.append((own, self.cnt[own]))
        byname = {}
        for w in waits:
            if w is None:
                continue
            sem, val = w
            if val > byname.get(sem, 0):
                byname[sem] = val
        wl = []
        wm = self.watermark[eng]
        for sem, val in byname.items():
            if val > wm.get(sem, 0):
                wm[sem] = val
                wl.append((sem, val))
        ms = None
        if inc is not None:
            self.cnt[inc] += inc_amt
            ms = (inc, self.cnt[inc])
        self.streams[eng].append((tuple(wl), fn, inc, inc_amt))
        return ms

    def now(self, sem):
        return (sem, self.cnt[sem])

    def play(self, block):
        sems = self.sems
        for engname in ENGS:
            stream = self.streams[engname]
            if not stream:
                continue

            def make(stream):
                def run(e):
                    for waits, fn, inc, inc_amt in stream:
                        for sem, val in waits:
                            e.wait_ge(sems[sem], val)
                        ins = fn(e)
                        if inc is not None:
                            ins.then_inc(sems[inc], inc_amt)
                return run
            getattr(block, engname)(make(stream))


class Z:
    pass


CF_COLS = 768
CB_COLS = 8


def host_consts():
    cf = np.zeros((128, CF_COLS), np.float32)
    col = {}
    c = 0

    def take(n, name):
        nonlocal c
        col[name] = c
        c += n
    take(1, "ones"); cf[:, col["ones"]] = 1.0
    take(1, "eps_ln"); cf[:, col["eps_ln"]] = 1e-5
    take(1, "eps_rms"); cf[:, col["eps_rms"]] = 1e-6
    take(1, "eps_l2"); cf[:, col["eps_l2"]] = 1e-12
    take(1, "zero")
    take(128, "ones_row"); cf[:, col["ones_row"]:col["ones_row"] + 128] = 1.0
    take(2, "seg64")
    cf[0:64, col["seg64"]] = 1.0
    cf[64:128, col["seg64"] + 1] = 1.0
    take(128, "ex2")
    cf[0, col["ex2"]:col["ex2"] + 64] = 1.0
    cf[1, col["ex2"] + 64:col["ex2"] + 128] = 1.0
    take(2, "segq")
    cf[0:16, col["segq"]] = 1.0
    cf[64:80, col["segq"] + 1] = 1.0
    take(128, "exq")
    cf[0, col["exq"]:col["exq"] + 16] = 1.0
    cf[1, col["exq"] + 64:col["exq"] + 80] = 1.0
    take(128, "ident")
    cf[:, col["ident"]:col["ident"] + 128] = np.eye(128, dtype=np.float32)
    assert c <= CF_COLS, c

    cb = np.zeros((128, CB_COLS), BF)
    colb = {}
    colb["seg64b"] = 0
    cb[0:64, 0] = BF(1.0)
    cb[64:128, 1] = BF(1.0)
    colb["segqb"] = 2
    cb[0:16, 2] = BF(1.0)
    cb[64:80, 3] = BF(1.0)
    colb["onesb"] = 4
    cb[:, 4] = BF(1.0)
    sel = np.zeros((8, 1024), BF)
    for e in range(8):
        sel[e, e * 128:(e + 1) * 128] = BF(1.0)
    cmask = np.zeros((128, 2048), BF)
    kk = np.arange(128)[:, None]
    qq = np.arange(512)[None, :]
    for ri, r in enumerate((0, 128, 256, 384)):
        cmask[:, ri * 512:(ri + 1) * 512] = (qq - kk - r >= 0).astype(BF)
    return cf, col, cb, colb, sel, cmask


CF, CFCOL, CB, CBCOL, SEL8H, CMASKH = host_consts()


CONV_SPECS = [
    ("l0a_wq", D, 256), ("l0a_wk", D, 256), ("l0a_wv", D, D), ("l0a_wout", D, D),
    ("l0f_wq", D, D), ("l0f_wdkv", D, LAT), ("l0f_wuk", LAT, D),
    ("l0f_wuv", LAT, D), ("l0f_wo", D, D), ("l0f_gate", D, 8),
    ("l0f_w1", EXP * D, EDIM), ("l0f_w2", EXP * EDIM, D),
    ("l1a_wqT", 64, 64), ("l1a_wkT", 64, 64), ("l1a_wvT", 64, 64),
    ("l1a_woT", D, D), ("l1f_wg", D, 8),
    ("l1f_w1", EXP * D, EDIM), ("l1f_w2", EXP * EDIM, D),
]


def build_conv():
    nc = bass.Bass("TRN2", target_bir_lowering=False)
    ins, outs = {}, {}
    for name, r, c in CONV_SPECS:
        ins[name] = nc.dram_tensor(name, [r, c], F32, kind="ExternalInput").ap()
        outs[name] = nc.dram_tensor("cv_" + name, [r, c], BF16, kind="ExternalOutput").ap()
    with ExitStack() as ctx:
        stg = [ctx.enter_context(nc.sbuf_tensor(f"stg{i}", [128, 4096], F32)) for i in range(2)]
        ob = [ctx.enter_context(nc.sbuf_tensor(f"ob{i}", [128, 4096], BF16)) for i in range(2)]
        P = Prog(nc, ctx)
        block = ctx.enter_context(nc.Block())
        free_ms = [None, None]
        st_ms = [None, None]
        slot = 0
        for name, r, c in CONV_SPECS:
            nchunk = max(1, r // 128)
            rows = min(r, 128)
            for i in range(nchunk):
                for j0 in range(0, c, 4096):
                    w = min(4096, c - j0)
                    sl = slot % 2
                    slot += 1
                    if r >= 128:
                        src = ins[name].rearrange("(n p) c -> n p c", p=128)[i, :, j0:j0 + w]
                        dst = outs[name].rearrange("(n p) c -> n p c", p=128)[i, :, j0:j0 + w]
                    else:
                        src = ins[name][:, j0:j0 + w]
                        dst = outs[name][:, j0:j0 + w]
                    ld = P.op("sync", lambda e, s=src, t=stg[sl], rr=rows, ww=w:
                              e.dma_start(out=t[0:rr, 0:ww], in_=s),
                              waits=[free_ms[sl]], inc=P.group(), inc_amt=16)
                    cv = P.op("vector", lambda e, t=stg[sl], o=ob[sl], rr=rows, ww=w:
                              e.tensor_copy(o[0:rr, 0:ww], t[0:rr, 0:ww]),
                              waits=[ld, st_ms[sl]], inc="DV")
                    free_ms[sl] = cv
                    st = P.op("sync", lambda e, o=ob[sl], d=dst, rr=rows, ww=w:
                              e.dma_start(out=d, in_=o[0:rr, 0:ww]),
                              waits=[cv], inc=P.group(), inc_amt=16)
                    st_ms[sl] = st
        P.op("sync", lambda e: e.nop(), waits=[st_ms[0], st_ms[1], P.now("DV")])
        P.play(block)
    return nc


ARENA_COLS = 52992
DBG = {"mode": None, "nexp": EXP, "nseq": None, "mla_qbs": None, "mla_heads": None}


def carve(nc, ctx):
    z = Z()
    t = ctx.enter_context(nc.sbuf_tensor("arena", [128, ARENA_COLS], F32))
    z.arena = t
    off = [0]

    def cv(cols, dtype=F32):
        a = t[:, off[0]:off[0] + cols]
        off[0] += cols
        return a.bitcast(BF16) if dtype == BF16 else a

    z.CSTF = cv(CF_COLS)
    z.CSTB = cv(CB_COLS // 2, BF16)
    z.ROWS = cv(2048)
    z.ROWS2 = cv(1024)
    z.SCR = cv(512)
    z.SCB = cv(512, BF16)
    z.g1c = cv(8)
    z.g2c = cv(8)
    z.b1f = cv(128)
    z.b2f = cv(64)
    z.X = cv(16384)
    z.H = cv(16384)
    z.pool0 = off[0]
    z.xoff = z.pool0 - 32768
    z.hoff = z.pool0 - 16384

    z.seg64b = z.CSTB[:, CBCOL["seg64b"]:CBCOL["seg64b"] + 2]
    z.segqb = z.CSTB[:, CBCOL["segqb"]:CBCOL["segqb"] + 2]
    z.onesb = z.CSTB[:, CBCOL["onesb"]:CBCOL["onesb"] + 1]
    z.ones_row = z.CSTF[0:1, CFCOL["ones_row"]:CFCOL["ones_row"] + 128]

    def ones_row_at(base):
        return z.CSTF[base:base + 1, CFCOL["ones_row"]:CFCOL["ones_row"] + 128]
    z.ones_row_at = ones_row_at
    z.seg64 = z.CSTF[:, CFCOL["seg64"]:CFCOL["seg64"] + 2]
    z.ex2 = z.CSTF[0:2, CFCOL["ex2"]:CFCOL["ex2"] + 128]
    z.segq = z.CSTF[:, CFCOL["segq"]:CFCOL["segq"] + 2]
    z.exq = z.CSTF[0:2, CFCOL["exq"]:CFCOL["exq"] + 128]
    z.ident = z.CSTF[:, CFCOL["ident"]:CFCOL["ident"] + 128]

    def cst1(name, rows=128):
        return z.CSTF[0:rows, CFCOL[name]:CFCOL[name] + 1]
    z.cst1 = cst1

    z.pb = [ctx.enter_context(nc.psum_tensor(f"pb{i}", [128, 512], F32))[:, :]
            for i in range(8)]
    z.bank_free = [None] * 8
    z.ms_scr_free = None
    z.ms_scb_free = None
    z.ms_scb2_free = None
    z.ms_scb_sq = None
    z.ms_gt_free = None
    z.ms_gt_ac = None
    z.ms_glq_free = None
    z.ms_cbs_free = None
    z.ms_rowtemp = None
    z.pt_free = [None, None]
    z.hid_guard = None
    return z


def pool(z, use):
    reg_lo = z.xoff if use == "x" else z.hoff
    state = {"alt": 0, "off": z.pool0}

    def cv(cols, dtype=F32):
        if state["alt"] + cols <= 16384:
            a = z.arena[:, reg_lo + state["alt"]: reg_lo + state["alt"] + cols]
            state["alt"] += cols
        else:
            assert state["off"] + cols <= ARENA_COLS, (state, cols)
            a = z.arena[:, state["off"]:state["off"] + cols]
            state["off"] += cols
        return a.bitcast(BF16) if dtype == BF16 else a
    return cv


def carve_attn(z, kind, use):
    cv = pool(z, use)
    if kind == "mla":
        z.kb = cv(8192, BF16)
        z.vb = cv(8320, BF16)
        z.xnq = cv(2048, BF16)
        z.wA = cv(2048, BF16)
        z.wBC = cv(2048, BF16)
        z.wD = cv(4096, BF16)
        r0 = cv(2560, BF16)
        z.cbuf = r0[:, 0:2048]
        z.ptb = r0[:, 0:1024]
        z.ao = r0[:, 1024:5120]
        z.cmask = cv(1024, BF16)
    elif kind == "a1":
        z.kb = cv(8192, BF16)
        z.vb = cv(8320, BF16)
        z.xnq = cv(2048, BF16)
        z.ptb = cv(512, BF16)
        z.ao = cv(2048, BF16)
        z.wD = cv(4096, BF16)
        z.wqT = cv(32, BF16)
        z.wkT = cv(32, BF16)
        z.wvT = cv(32, BF16)
        z.cmask = cv(1024, BF16)
    elif kind == "a0":
        z.kb16 = cv(8192, BF16)
        z.vb = cv(8320, BF16)
        z.xnq = cv(2048, BF16)
        z.ptb = cv(512, BF16)
        z.ao = cv(2048, BF16)
        z.wQ0 = cv(1024, BF16)
        z.wK0 = cv(1024, BF16)
        z.wV0 = cv(4096, BF16)
        z.wD = z.wV0
        z.GT = cv(64)


def carve_moe(z, use):
    cv = pool(z, use)
    z.xn = cv(8192, BF16)
    z.wE1 = cv(8192, BF16)
    z.wE2 = cv(8192, BF16)
    z.hid = cv(4096, BF16)
    z.wG = cv(64, BF16)
    z.GLQ = cv(512)[0:8, :]
    z.CFM = cv(1024, BF16)[0:8, :]
    z.GT = cv(64)
    z.CBS = cv(512)
    z.MSCR = z.SCR
    z.sel8 = cv(512, BF16)[0:8, :]


def xsl(buf, fci, t0, n):
    return buf[:, fci * S + t0: fci * S + t0 + n]


def qsl(buf, fci):
    return buf[:, fci * QB:(fci + 1) * QB]


def emit_weight_load(P, dst_sb, dram, K, M, waits=(), queue="sync"):
    kc = max(1, K // 128)
    rows = min(K, 128)
    sem = P.group()
    ms = None
    src3 = dram.rearrange("(n p) c -> n p c", p=128) if K >= 128 else None
    for i in range(kc):
        src = src3[i] if K >= 128 else dram
        ms = P.op(queue, lambda e, s=src, d=dst_sb, i=i, rr=rows, M=M:
                  e.dma_start(out=d[0:rr, i * M:(i + 1) * M], in_=s),
                  waits=(waits if i == 0 else ()), inc=sem, inc_amt=16)
    return ms


def emit_ln_rows(P, z, src, *, rms=False, eps):
    for qb in range(NQB):
        t0 = qb * QB
        sx_ms = sxx_ms = None
        for fci in range(FC):
            cvt = P.op("vector", lambda e, fci=fci, t0=t0:
                       e.tensor_copy(z.SCB[:, 0:QB], xsl(src, fci, t0, QB)),
                       waits=[z.ms_scb_free, z.ms_scb_sq], inc="DV")
            sq = P.op("scalar", lambda e:
                      e.activation(z.SCB[:, QB:2 * QB], z.SCB[:, 0:QB], AF.Square),
                      waits=[cvt, z.ms_scb2_free], inc="AC")
            z.ms_scb_sq = sq
            if not rms:
                sx_ms = P.op("tensor", lambda e, fci=fci:
                             e.matmul(z.pb[6][0:1, :], z.onesb, z.SCB[:, 0:QB],
                                      start=(fci == 0), stop=(fci == FC - 1)),
                             waits=([cvt, z.bank_free[6]] if fci == 0 else [cvt]),
                             inc="PE")
            sxx_ms = P.op("tensor", lambda e, fci=fci:
                          e.matmul(z.pb[7][0:1, :], z.onesb, z.SCB[:, QB:2 * QB],
                                   start=(fci == 0), stop=(fci == FC - 1)),
                          waits=([sq, z.bank_free[7]] if fci == 0 else [sq]),
                          inc="PE")
            z.ms_scb_free = sxx_ms if rms else sx_ms
            z.ms_scb2_free = sxx_ms
        tA = z.ROWS2[0:1, 0:QB]
        tB = z.ROWS2[0:1, 512:512 + QB]
        tC = z.SCR[0:1, 0:QB]
        rr = lambda base, t0=t0: z.ROWS[base:base + 1, t0:t0 + QB]
        if not rms:
            m1 = P.op("vector", lambda e: e.tensor_scalar(
                tA, z.pb[6][0:1, :], 1.0 / D, None, OP.mult),
                waits=[sx_ms, z.ms_rowtemp], inc="DV")
            e1 = P.op("vector", lambda e: e.tensor_scalar(
                tB, z.pb[7][0:1, :], 1.0 / D, None, OP.mult),
                waits=[sxx_ms, z.ms_rowtemp], inc="DV")
            z.bank_free[6] = m1
            z.bank_free[7] = e1
            v1 = P.op("vector", lambda e: e.tensor_tensor(
                tC, tA, tA, OP.mult), waits=[m1, z.ms_scr_free], inc="DV")
            v2 = P.op("vector", lambda e: e.tensor_tensor(
                tB, tB, tC, OP.subtract), waits=[v1, e1], inc="DV")
            l1 = P.op("scalar", lambda e: e.activation(
                tB, tB, AF.Ln, bias=z.cst1(eps, rows=1)), waits=[v2], inc="AC")
            l2 = P.op("scalar", lambda e, rr=rr: e.activation(
                rr(0), tB, AF.Exp, scale=-0.5), waits=[l1], inc="AC")
            v3 = P.op("vector", lambda e, rr=rr: e.tensor_tensor(
                tC, tA, rr(0), OP.mult), waits=[l2], inc="DV")
            z.ms_scr_free = v3
            mv = P.op("tensor", lambda e: e.matmul(
                z.pb[6][:, :], z.ones_row_at(0), tC, start=True, stop=True),
                waits=[v3, z.bank_free[6]], inc="PE")
            mvc = P.op("vector", lambda e, rr=rr: e.tensor_copy(
                rr(32), z.pb[6][32:33, :]), waits=[mv], inc="DV")
            z.bank_free[6] = mvc
            z.ms_rowtemp = mvc
        else:
            e1 = P.op("vector", lambda e: e.tensor_scalar(
                tB, z.pb[7][0:1, :], 1.0 / D, None, OP.mult),
                waits=[sxx_ms, z.ms_rowtemp], inc="DV")
            z.bank_free[7] = e1
            l1 = P.op("scalar", lambda e: e.activation(
                tB, tB, AF.Ln, bias=z.cst1(eps, rows=1)), waits=[e1], inc="AC")
            l2 = P.op("scalar", lambda e, rr=rr: e.activation(
                rr(0), tB, AF.Exp, scale=-0.5), waits=[l1], inc="AC")
            z.ms_rowtemp = l2
    return P.now("DV"), P.now("AC")


def emit_norm_qb(P, z, src, dst_slicer, t0, *, kind, g_col=None, row_waits=()):
    rsrc = 0
    b6 = P.op("tensor", lambda e: e.matmul(
        z.pb[6][:, :], z.ones_row_at(rsrc), z.ROWS[rsrc:rsrc + 1, t0:t0 + QB],
        start=True, stop=True),
        waits=list(row_waits) + [z.bank_free[6]], inc="PE")
    if kind == "ln":
        b7 = P.op("tensor", lambda e: e.matmul(
            z.pb[7][:, :], z.ones_row_at(32), z.ROWS[32:33, t0:t0 + QB],
            start=True, stop=True),
            waits=list(row_waits) + [z.bank_free[7]], inc="PE")
    last = None
    for fci in range(FC):
        p1 = P.op("vector", lambda e, fci=fci: e.tensor_tensor(
            z.SCR, xsl(src, fci, t0, QB), z.pb[6][:, :], OP.mult),
            waits=[b6, z.ms_scr_free], inc="DV")
        if kind == "ln":
            p2 = P.op("vector", lambda e, fci=fci: e.tensor_tensor(
                dst_slicer(fci), z.SCR, z.pb[7][:, :], OP.subtract),
                waits=[p1, b7], inc="DV")
        else:
            p2 = P.op("vector", lambda e, fci=fci: e.tensor_scalar(
                dst_slicer(fci), z.SCR, g_col(fci), None, OP.mult),
                waits=[p1], inc="DV")
        z.ms_scr_free = p2
        last = p2
    z.bank_free[6] = last
    if kind == "ln":
        z.bank_free[7] = last
    return last


def emit_sqmax(P, z, src_slicer, nchunks, segb, dst_col, *, init, bank=6, ready=()):
    run = None
    for i in range(nchunks):
        sq = P.op("scalar", lambda e, i=i:
                  e.activation(z.SCB[:, 0:QB], src_slicer(i), AF.Square),
                  waits=[z.ms_scb_free] + list(ready), inc="AC")
        mmq = P.op("tensor", lambda e: e.matmul(
            z.pb[bank][0:2, :], segb, z.SCB[:, 0:QB], start=True, stop=True),
            waits=[sq, z.bank_free[bank]], inc="PE")
        z.ms_scb_free = mmq
        rmx = P.op("vector", lambda e: e.tensor_reduce(
            z.ROWS2[0:2, dst_col + 1:dst_col + 2], z.pb[bank][0:2, :], axis=AX.X, op=OP.max),
            waits=[mmq], inc="DV")
        z.bank_free[bank] = rmx
        if init and i == 0:
            run = P.op("vector", lambda e: e.tensor_copy(
                z.ROWS2[0:2, dst_col:dst_col + 1], z.ROWS2[0:2, dst_col + 1:dst_col + 2]),
                waits=[rmx], inc="DV")
        else:
            run = P.op("vector", lambda e: e.tensor_tensor(
                z.ROWS2[0:2, dst_col:dst_col + 1], z.ROWS2[0:2, dst_col:dst_col + 1],
                z.ROWS2[0:2, dst_col + 1:dst_col + 2], OP.max),
                waits=[rmx, run], inc="DV")
    return run


def emit_bias_from_sqmax(P, z, mk_ms, mq_ms, *, bank=6):
    tp1 = P.op("tensor", lambda e: e.transpose(
        z.pb[bank][0:1, 0:2], z.ROWS2[0:2, 8:9], z.ident[0:2, 0:2]),
        waits=[mk_ms, z.bank_free[bank]], inc="PE")
    tp2 = P.op("tensor", lambda e: e.transpose(
        z.pb[bank][0:1, 2:4], z.ROWS2[0:2, 12:13], z.ident[0:2, 0:2]),
        waits=[mq_ms], inc="PE")
    r1 = P.op("vector", lambda e: e.tensor_reduce(
        z.ROWS2[0:1, 20:21], z.pb[bank][0:1, 0:2], axis=AX.X, op=OP.max),
        waits=[tp1, tp2], inc="DV")
    r2 = P.op("vector", lambda e: e.tensor_reduce(
        z.ROWS2[0:1, 21:22], z.pb[bank][0:1, 2:4], axis=AX.X, op=OP.max),
        waits=[tp1, tp2], inc="DV")
    r3 = P.op("vector", lambda e: e.tensor_tensor(
        z.ROWS2[0:1, 20:21], z.ROWS2[0:1, 20:21], z.ROWS2[0:1, 21:22], OP.mult),
        waits=[r1, r2], inc="DV")
    r4 = P.op("scalar", lambda e: e.activation(
        z.ROWS2[0:1, 20:21], z.ROWS2[0:1, 20:21], AF.Ln, bias=z.cst1("eps_l2", rows=1)),
        waits=[r3], inc="AC")
    r5 = P.op("scalar", lambda e: e.activation(
        z.ROWS2[0:1, 20:21], z.ROWS2[0:1, 20:21], AF.Exp, scale=0.5),
        waits=[r4], inc="AC")
    r6 = P.op("scalar", lambda e: e.mul(
        z.ROWS2[0:1, 20:21], z.ROWS2[0:1, 20:21], -0.125), waits=[r5], inc="AC")
    bb = P.op("tensor", lambda e: e.matmul(
        z.pb[bank][:, 0:1], z.ones_row, z.ROWS2[0:1, 20:21], start=True, stop=True),
        waits=[r6], inc="PE")
    bias_ms = P.op("vector", lambda e: e.tensor_copy(z.ROWS2[:, 16:17], z.pb[bank][:, 0:1]),
                   waits=[bb], inc="DV")
    z.bank_free[bank] = bias_ms
    return bias_ms, z.ROWS2[:, 16:17]


def emit_attn_qb(P, z, *, qb, q_src, kb, vb, bias, bias_ms, q_ready, kv_ready,
                 out_target, wout, wout_ready, scale=None, wout_wM=D):
    n_kc = 4 * qb + 4
    ao_last = None
    nheads = DBG["mla_heads"] or HEADS
    for h in range(nheads):
        ch, sl = h // 2, (h % 2) * 64
        for kc in range(n_kc):
            sbank = kc % 2
            smm = P.op("tensor", lambda e, kc=kc, ch=ch, sl=sl, sbank=sbank:
                       e.matmul(z.pb[sbank][:, :],
                                xsl(kb, ch, kc * 128, 128)[sl:sl + 64, :],
                                q_src(ch)[sl:sl + 64, :],
                                start=True, stop=True),
                       waits=[q_ready, kv_ready, z.bank_free[sbank]], inc="PE")
            pt = z.ptb[:, (kc % 2) * QB:(kc % 2) * QB + QB]
            ex = P.op("scalar", lambda e, pt=pt, sbank=sbank:
                      e.activation(pt, z.pb[sbank][:, :], AF.Exp, bias=bias, scale=0.125),
                      waits=[smm, bias_ms, z.pt_free[kc % 2]], inc="AC")
            z.bank_free[sbank] = ex
            lastp = ex
            if kc >= 4 * qb:
                ri = kc - 4 * qb
                lastp = P.op("vector", lambda e, pt=pt, ri=ri:
                             e.tensor_tensor(pt, pt, z.cmask[:, ri * QB:(ri + 1) * QB], OP.mult),
                             waits=[ex, z.cmask_ms], inc="DV")
            av_waits = [lastp] + ([z.bank_free[2]] if kc == 0 else [])
            avm = P.op("tensor", lambda e, kc=kc, h=h, pt=pt, n_kc=n_kc:
                       e.matmul(z.pb[2][0:65, :],
                                vb[:, (kc * HEADS + h) * 65:(kc * HEADS + h) * 65 + 65],
                                pt, start=(kc == 0), stop=(kc == n_kc - 1)),
                       waits=av_waits, inc="PE")
            z.pt_free[kc % 2] = avm
        rc = P.op("vector", lambda e: e.reciprocal(z.ROWS2[64:65, QB:2 * QB],
                                                   z.pb[2][64:65, :]),
                  waits=[avm], inc="DV")
        bc = P.op("tensor", lambda e: e.matmul(
            z.pb[3][:, :], z.ones_row_at(64), z.ROWS2[64:65, QB:2 * QB],
            start=True, stop=True),
            waits=[rc, z.bank_free[3]], inc="PE")
        bce = P.op("vector", lambda e: e.tensor_copy(z.SCR[0:64, 0:QB], z.pb[3][0:64, :]),
                   waits=[bc, z.ms_scr_free], inc="DV")
        z.bank_free[3] = bce
        ao = P.op("vector", lambda e, ch=ch, sl=sl:
                  e.tensor_tensor(z.ao[sl:sl + 64, ch * QB:(ch + 1) * QB],
                                  z.pb[2][0:64, :], z.SCR[0:64, 0:QB], OP.mult),
                  waits=[bce], inc="DV")
        z.ms_scr_free = ao
        z.bank_free[2] = ao
        ao_last = ao
    last = None
    if nheads < HEADS:
        return None
    for fci in range(FC):
        bank = 4 + fci % 2
        waits = [wout_ready, ao_last, z.bank_free[bank]]
        mm = None
        for c in range(FC):
            mm = P.op("tensor", lambda e, c=c, fci=fci, bank=bank:
                      e.matmul(z.pb[bank][:, :],
                               wout[:, c * wout_wM + fci * 128: c * wout_wM + fci * 128 + 128],
                               z.ao[:, c * QB:(c + 1) * QB],
                               start=(c == 0), stop=(c == FC - 1)),
                      waits=(waits if c == 0 else ()), inc=("PE" if c == FC - 1 else None))
        if scale is None:
            ev = P.op("vector", lambda e, fci=fci, qb=qb, bank=bank:
                      e.tensor_tensor(xsl(out_target, fci, qb * QB, QB),
                                      z.pb[bank][:, :], xsl(out_target, fci, qb * QB, QB),
                                      OP.add),
                      waits=[mm], inc="DV")
        else:
            ev = P.op("vector", lambda e, fci=fci, qb=qb, bank=bank:
                      e.scalar_tensor_tensor(xsl(out_target, fci, qb * QB, QB),
                                             z.pb[bank][:, :], scale,
                                             xsl(out_target, fci, qb * QB, QB),
                                             OP.mult, OP.add),
                      waits=[mm], inc="DV")
        z.bank_free[bank] = ev
        last = ev
    return last


def emit_mla(P, z, w):
    wdkv_ms = emit_weight_load(P, z.wA, w["wdkv"], D, LAT, waits=z.pool_free)
    cm_ms = P.op("sync", lambda e: e.dma_start(out=z.cmask, in_=P.cmask_dram),
                 waits=z.pool_free, inc=P.group(), inc_amt=16)
    z.cmask_ms = cm_ms
    ones_ms = P.op("gpsimd", lambda e: e.memset(
        z.vb.rearrange("p (k e) -> p k e", e=65)[:, :, 64:65], 1.0),
        waits=z.pool_free_g, inc="PL")
    mk_ms = None
    for qb in range(DBG["mla_qbs"] or NQB):
        t0 = qb * QB
        xn_ms = emit_norm_qb(P, z, z.H, lambda fci: qsl(z.xnq, fci), t0,
                             kind="rms", g_col=lambda fci: z.g1c[:, fci:fci + 1])
        c_ms = None
        for mc in range(4):
            bank = mc % 2
            mm = None
            for kc in range(FC):
                mm = P.op("tensor", lambda e, kc=kc, mc=mc, bank=bank:
                          e.matmul(z.pb[bank][:, :],
                                   z.wA[:, kc * LAT + mc * 128: kc * LAT + mc * 128 + 128],
                                   qsl(z.xnq, kc), start=(kc == 0), stop=(kc == FC - 1)),
                          waits=([wdkv_ms, xn_ms, z.bank_free[bank]] if kc == 0 else ()),
                          inc=("PE" if kc == FC - 1 else None))
            c_ms = P.op("vector", lambda e, mc=mc, bank=bank:
                        e.tensor_copy(qsl(z.cbuf, mc), z.pb[bank][:, :]),
                        waits=[mm], inc="DV")
            z.bank_free[bank] = c_ms
        wuk_ms = emit_weight_load(P, z.wBC, w["wuk"], LAT, D, waits=[P.now("PE")])
        k_ms = None
        for mc in range(FC):
            bank = 2 + mc % 2
            mm = None
            for kc in range(4):
                mm = P.op("tensor", lambda e, kc=kc, mc=mc, bank=bank:
                          e.matmul(z.pb[bank][:, :],
                                   z.wBC[:, kc * D + mc * 128: kc * D + mc * 128 + 128],
                                   qsl(z.cbuf, kc), start=(kc == 0), stop=(kc == 3)),
                          waits=([wuk_ms, c_ms, z.bank_free[bank]] if kc == 0 else ()),
                          inc=("PE" if kc == 3 else None))
            k_ms = P.op("vector", lambda e, mc=mc, bank=bank, t0=t0:
                        e.tensor_copy(xsl(z.kb, mc, t0, QB), z.pb[bank][:, :]),
                        waits=[mm], inc="DV")
            z.bank_free[bank] = k_ms
        wuv_ms = emit_weight_load(P, z.wBC, w["wuv"], LAT, D, waits=[P.now("PE")])
        for tt in range(4):
            kc_abs = qb * 4 + tt
            for vh in range(2):
                bank = 4 + (tt * 2 + vh) % 2
                mm = None
                for lc in range(4):
                    mm = P.op("tensor", lambda e, lc=lc, tt=tt, vh=vh, bank=bank:
                              e.matmul(z.pb[bank][:, :],
                                       qsl(z.cbuf, lc)[:, tt * 128:tt * 128 + 128],
                                       z.wBC[:, lc * D + vh * 512: lc * D + vh * 512 + 512],
                                       start=(lc == 0), stop=(lc == 3)),
                              waits=([c_ms, wuv_ms, ones_ms, z.bank_free[bank]]
                                     if lc == 0 else ()),
                              inc=("PE" if lc == 3 else None))
                dst = z.vb[:, (kc_abs * HEADS + vh * 8) * 65:(kc_abs * HEADS + vh * 8 + 8) * 65]
                dst = dst.rearrange("p (h e) -> p h e", e=65)[:, :, 0:64]
                v_ms = P.op("vector", lambda e, dst=dst, bank=bank:
                            e.tensor_copy(dst,
                                          z.pb[bank][:, :].rearrange("p (h e) -> p h e", e=64)),
                            waits=[mm], inc="DV")
                z.bank_free[bank] = v_ms
        mk_ms = emit_sqmax(P, z, lambda i, t0=t0: xsl(z.kb, i, t0, QB), FC,
                           z.seg64b, 8, init=(qb == 0), ready=[P.now("DV")])
        wq_ms = emit_weight_load(P, z.wD, w["wq"], D, D, waits=[P.now("PE")])
        mms = []
        for mc in range(FC):
            mm = None
            for kc in range(FC):
                mm = P.op("tensor", lambda e, kc=kc, mc=mc:
                          e.matmul(z.pb[mc][:, :],
                                   z.wD[:, kc * D + mc * 128: kc * D + mc * 128 + 128],
                                   qsl(z.xnq, kc), start=(kc == 0), stop=(kc == FC - 1)),
                          waits=([wq_ms, xn_ms, z.bank_free[mc]] if kc == 0 else ()),
                          inc=("PE" if kc == FC - 1 else None))
            mms.append(mm)
        qev = None
        for mc in range(FC):
            qev = P.op("vector", lambda e, mc=mc:
                       e.tensor_copy(qsl(z.xnq, mc), z.pb[mc][:, :]),
                       waits=[mms[-1]], inc="DV")
            z.bank_free[mc] = qev
        mq_ms = emit_sqmax(P, z, lambda i: qsl(z.xnq, i), FC, z.seg64b, 12, init=True,
                           ready=[P.now("DV")])
        bias_ms, bias = emit_bias_from_sqmax(P, z, mk_ms, mq_ms)
        wo_ms = emit_weight_load(P, z.wD, w["wo"], D, D, waits=[P.now("PE")])
        emit_attn_qb(P, z, qb=qb, q_src=lambda ch: qsl(z.xnq, ch),
                     kb=z.kb, vb=z.vb, bias=bias, bias_ms=bias_ms,
                     q_ready=P.now("DV"), kv_ready=P.now("DV"),
                     out_target=z.H, wout=z.wD, wout_ready=wo_ms, scale=None)


def emit_moe(P, z, w, *, variant, target):
    l1 = variant == "l1"
    gate_ms = emit_weight_load(P, z.wG, w["gate"], D, 8, waits=z.pool_free)
    sel_ms = P.op("sync", lambda e: e.dma_start(out=z.sel8, in_=P.sel8_dram),
                  waits=z.pool_free, inc=P.group(), inc_amt=16)
    for qb in range(NQB):
        if not l1:
            emit_norm_qb(P, z, z.H,
                         lambda fci, qb=qb: z.xn[:, fci * S + qb * QB: fci * S + qb * QB + QB],
                         qb * QB, kind="rms", g_col=lambda fci: z.g2c[:, fci:fci + 1])
        else:
            emit_norm_qb(P, z, z.X,
                         lambda fci, qb=qb: z.xn[:, fci * S + qb * QB: fci * S + qb * QB + QB],
                         qb * QB, kind="ln")
    xn_all = P.now("DV")
    cfm_ms = None
    for qb in range(NQB):
        mm = None
        for kc in range(FC):
            mm = P.op("tensor", lambda e, kc=kc, qb=qb:
                      e.matmul(z.pb[6][0:8, :],
                               z.wG[:, kc * 8: kc * 8 + 8],
                               z.xn[:, kc * S + qb * QB: kc * S + qb * QB + QB],
                               start=(kc == 0), stop=(kc == FC - 1)),
                      waits=([gate_ms, xn_all, z.bank_free[6]] if kc == 0 else ()),
                      inc=("PE" if kc == FC - 1 else None))
        gl_ms = P.op("vector", lambda e, qb=qb:
                     e.tensor_copy(z.GLQ, z.pb[6][0:8, :]),
                     waits=[mm, z.ms_glq_free], inc="DV")
        z.bank_free[6] = gl_ms
        if DBG["mode"] == "gatemm":
            continue
        for ttl in range(4):
            tt = qb * 4 + ttl
            tp = P.op("tensor", lambda e, ttl=ttl: e.transpose(
                z.pb[7][:, 0:8], z.GLQ[:, ttl * 128:(ttl + 1) * 128], z.ident[0:8, 0:8]),
                waits=[gl_ms, z.bank_free[7]], inc="PE")
            z.ms_glq_free = tp
            gt = P.op("vector", lambda e: e.tensor_copy(z.GT[:, 0:8], z.pb[7][:, 0:8]),
                      waits=[tp, z.ms_gt_free, z.ms_gt_ac], inc="DV")
            z.bank_free[7] = gt
            z.ms_gt_free = gt
            if DBG["mode"] == "gatetp":
                continue
            col = lambda c: z.GT[:, c:c + 1]
            blk = lambda c0: z.GT[:, c0:c0 + 8]
            if not l1:
                m1r = P.op("vector", lambda e, col=col, blk=blk: e.tensor_reduce(
                    col(8), blk(0), axis=AX.X, op=OP.max), waits=[gt], inc="DV")
                P.op("vector", lambda e, col=col, blk=blk: e.tensor_scalar(
                    blk(16), blk(0), col(8), 0.0, OP.subtract, OP.is_ge),
                    waits=[m1r], inc="DV")
                P.op("vector", lambda e, col=col, blk=blk: e.scalar_tensor_tensor(
                    blk(24), blk(16), -1e30, blk(0), OP.mult, OP.add), inc="DV")
                m2r = P.op("vector", lambda e, col=col, blk=blk: e.tensor_reduce(
                    col(9), blk(24), axis=AX.X, op=OP.max), inc="DV")
                dd = P.op("vector", lambda e, col=col: e.tensor_tensor(
                    col(10), col(9), col(8), OP.subtract), inc="DV")
                ee = P.op("scalar", lambda e, col=col: e.activation(col(11), col(10), AF.Exp),
                          waits=[dd], inc="AC")
                z.ms_gt_ac = ee
                dn = P.op("scalar", lambda e, col=col: e.add(
                    col(12), col(11), 1.0), waits=[ee], inc="AC")
                rdr = P.op("vector", lambda e, col=col: e.reciprocal(col(13), col(12)),
                           waits=[dn], inc="DV")
                c2r = P.op("vector", lambda e, col=col: e.tensor_tensor(
                    col(14), col(11), col(13), OP.mult), inc="DV")
                P.op("vector", lambda e, col=col, blk=blk: e.tensor_scalar(
                    blk(32), blk(24), col(9), 0.0, OP.subtract, OP.is_ge),
                    waits=[m2r], inc="DV")
                P.op("vector", lambda e, col=col, blk=blk: e.tensor_scalar(
                    blk(40), blk(16), col(13), None, OP.mult), waits=[rdr], inc="DV")
                P.op("vector", lambda e, col=col, blk=blk: e.tensor_scalar(
                    blk(48), blk(32), col(14), None, OP.mult), waits=[c2r], inc="DV")
                ct = P.op("vector", lambda e, blk=blk: e.tensor_tensor(
                    blk(40), blk(40), blk(48), OP.add), inc="DV")
            else:
                nm = P.op("vector", lambda e, col=col, blk=blk: e.tensor_reduce(
                    col(9), blk(0), axis=AX.X, op=OP.max, negate=True),
                    waits=[gt], inc="DV")
                eL = P.op("scalar", lambda e, col=col, blk=blk: e.activation(
                    blk(16), blk(0), AF.Exp, bias=col(9)), waits=[nm], inc="AC")
                sr = P.op("vector", lambda e, col=col, blk=blk: e.tensor_reduce(
                    col(10), blk(16), axis=AX.X, op=OP.add), waits=[eL], inc="DV")
                rsr = P.op("vector", lambda e, col=col: e.reciprocal(col(11), col(10)),
                           waits=[sr], inc="DV")
                P.op("vector", lambda e, col=col, blk=blk: e.tensor_scalar(
                    blk(24), blk(16), col(11), None, OP.mult), waits=[rsr], inc="DV")
                m12 = P.op("vector", lambda e, col=col, blk=blk: e.tensor_reduce(
                    col(12), blk(24), axis=AX.X, op=OP.max), inc="DV")
                P.op("vector", lambda e, col=col, blk=blk: e.tensor_scalar(
                    blk(32), blk(24), col(12), 0.0, OP.subtract, OP.is_ge),
                    waits=[m12], inc="DV")
                P.op("vector", lambda e, col=col, blk=blk: e.scalar_tensor_tensor(
                    blk(40), blk(32), -1e30, blk(24), OP.mult, OP.add), inc="DV")
                m13 = P.op("vector", lambda e, col=col, blk=blk: e.tensor_reduce(
                    col(13), blk(40), axis=AX.X, op=OP.max), inc="DV")
                P.op("vector", lambda e, col=col, blk=blk: e.tensor_scalar(
                    blk(48), blk(40), col(13), 0.0, OP.subtract, OP.is_ge),
                    waits=[m13], inc="DV")
                P.op("vector", lambda e, col=col, blk=blk: e.tensor_scalar(
                    blk(40), blk(32), col(12), None, OP.mult), waits=[m12], inc="DV")
                P.op("vector", lambda e, col=col, blk=blk: e.tensor_scalar(
                    blk(48), blk(48), col(13), None, OP.mult), waits=[m13], inc="DV")
                ct = P.op("vector", lambda e, blk=blk: e.tensor_tensor(
                    blk(40), blk(40), blk(48), OP.add), inc="DV")
            z.ms_gt_ac = P.now("AC")
            tpc = P.op("tensor", lambda e: e.transpose(
                z.pb[7][0:8, 0:128], z.GT[:, 40:48], z.ident),
                waits=[ct, z.bank_free[7]], inc="PE")
            cfm_ms = P.op("vector", lambda e, tt=tt: e.tensor_copy(
                z.CFM[0:8, tt * 128:(tt + 1) * 128], z.pb[7][0:8, 0:128]),
                waits=[tpc], inc="DV")
            z.bank_free[7] = cfm_ms
            z.ms_gt_free = cfm_ms
    if DBG["mode"] == "gatec":
        P.op("vector", lambda e: e.tensor_copy(
            xsl(z.X, 0, 0, 2048)[0:8, :].bitcast(BF16)[:, 0:2048], z.CFM[0:8, :]),
            waits=[cfm_ms], inc="DV")
        P.op("vector", lambda e: e.tensor_copy(xsl(z.X, 1, 0, 64), z.GT),
            waits=[cfm_ms], inc="DV")
        return P.now("DV")
    if DBG["mode"] in ("gate", "gatemm", "gatetp"):
        return P.now("DV")
    w1_3 = w["w1"]
    w2_3 = w["w2"]
    w1_ms = emit_weight_load(P, z.wE1, w1_3[0:D, :], D, EDIM, waits=z.pool_free)
    ye_prev = None
    for e_i in range(DBG["nexp"]):
        w2_ms = emit_weight_load(P, z.wE2, w2_3[e_i * EDIM:(e_i + 1) * EDIM, :], EDIM, D,
                                 waits=[ye_prev] + list(z.pool_free))
        for tq in range(4):
            for hcq in range(16):
                bank = (tq * 16 + hcq) % 4
                mm = None
                for kc in range(FC):
                    mm = P.op("tensor", lambda e, kc=kc, hcq=hcq, tq=tq, bank=bank:
                              e.matmul(z.pb[bank][:, :],
                                       z.wE1[:, kc * EDIM + hcq * 128: kc * EDIM + hcq * 128 + 128],
                                       z.xn[:, kc * S + tq * QB: kc * S + tq * QB + QB],
                                       start=(kc == 0), stop=(kc == FC - 1)),
                              waits=([w1_ms, xn_all, z.bank_free[bank]] if kc == 0 else ()),
                              inc=("PE" if kc == FC - 1 else None))
                if l1:
                    ev = P.op("scalar", lambda e, hcq=hcq, bank=bank, e_i=e_i:
                              e.activation(z.hid[:, hcq * QB:hcq * QB + QB],
                                           z.pb[bank][:, :], AF.Relu,
                                           bias=z.b1f[:, e_i * 16 + hcq: e_i * 16 + hcq + 1]),
                              waits=[mm, z.hid_guard], inc="AC")
                else:
                    ev = P.op("scalar", lambda e, hcq=hcq, bank=bank:
                              e.activation(z.hid[:, hcq * QB:hcq * QB + QB],
                                           z.pb[bank][:, :], AF.Silu),
                              waits=[mm, z.hid_guard], inc="AC")
                z.bank_free[bank] = ev
            cb_mm = P.op("tensor", lambda e, e_i=e_i, tq=tq: e.matmul(
                z.pb[6][:, :], z.sel8[:, e_i * 128:(e_i + 1) * 128],
                z.CFM[0:8, tq * QB:(tq + 1) * QB], start=True, stop=True),
                waits=[cfm_ms, sel_ms, z.bank_free[6]], inc="PE")
            cb_ms = P.op("vector", lambda e: e.tensor_copy(z.CBS, z.pb[6][:, :]),
                         waits=[cb_mm, z.ms_cbs_free], inc="DV")
            z.bank_free[6] = cb_ms
            hid_q = P.now("AC")
            p2 = None
            for fci in range(FC):
                bank = 4 + fci % 2
                mm = None
                for hcq in range(16):
                    mm = P.op("tensor", lambda e, hcq=hcq, fci=fci, bank=bank:
                              e.matmul(z.pb[bank][:, :],
                                       z.wE2[:, hcq * D + fci * 128: hcq * D + fci * 128 + 128],
                                       z.hid[:, hcq * QB:hcq * QB + QB],
                                       start=(hcq == 0), stop=(hcq == 15)),
                              waits=([w2_ms, hid_q, z.bank_free[bank]] if hcq == 0 else ()),
                              inc=("PE" if hcq == 15 else None))
                if l1:
                    p1 = P.op("vector", lambda e, fci=fci, e_i=e_i, bank=bank:
                              e.scalar_tensor_tensor(z.MSCR, z.pb[bank][:, :],
                                                     z.b2f[:, e_i * 8 + fci: e_i * 8 + fci + 1],
                                                     z.CBS, OP.add, OP.mult),
                              waits=[mm, cb_ms, z.ms_scr_free], inc="DV")
                    p2 = P.op("vector", lambda e, fci=fci, tq=tq:
                              e.scalar_tensor_tensor(xsl(target, fci, tq * QB, QB),
                                                     z.MSCR, 0.5,
                                                     xsl(target, fci, tq * QB, QB),
                                                     OP.mult, OP.add),
                              waits=[p1], inc="DV")
                else:
                    p1 = P.op("vector", lambda e, fci=fci, bank=bank:
                              e.tensor_tensor(z.MSCR, z.pb[bank][:, :], z.CBS, OP.mult),
                              waits=[mm, cb_ms, z.ms_scr_free], inc="DV")
                    p2 = P.op("vector", lambda e, fci=fci, tq=tq:
                              e.tensor_tensor(xsl(target, fci, tq * QB, QB),
                                              z.MSCR, xsl(target, fci, tq * QB, QB), OP.add),
                              waits=[p1], inc="DV")
                z.ms_scr_free = p2
                z.bank_free[bank] = p2
            z.ms_cbs_free = p2
            z.hid_guard = P.now("PE")
        ye_prev = P.now("PE")
        if e_i + 1 < DBG["nexp"]:
            w1_ms = emit_weight_load(P, z.wE1, w1_3[(e_i + 1) * D:(e_i + 2) * D, :], D, EDIM,
                                     waits=[P.now("PE")])
    return P.now("DV")


def emit_attn1(P, z, w):
    def load2(dst, dram):
        g = P.group()
        P.op("sync", lambda e: e.dma_start(out=dst[0:64, 0:64], in_=dram),
             waits=z.pool_free, inc=g, inc_amt=16)
        return P.op("sync", lambda e: e.dma_start(out=dst[64:128, 0:64], in_=dram),
                    waits=z.pool_free, inc=g, inc_amt=16)
    wq_ms = load2(z.wqT, w["wqT"])
    cm_ms = P.op("sync", lambda e: e.dma_start(out=z.cmask, in_=P.cmask_dram),
                 waits=z.pool_free, inc=P.group(), inc_amt=16)
    z.cmask_ms = cm_ms
    wk_ms = load2(z.wkT, w["wkT"])
    wv_ms = load2(z.wvT, w["wvT"])
    emit_ln_rows(P, z, z.X, rms=False, eps="eps_ln")
    wo_ms = emit_weight_load(P, z.wD, w["woT"], D, D, waits=z.pool_free)
    ones_ms = P.op("gpsimd", lambda e: e.memset(
        z.vb.rearrange("p (k e) -> p k e", e=65)[:, :, 64:65], 1.0),
        waits=z.pool_free_g, inc="PL")
    mk_ms = None
    for qb in range(NQB):
        t0 = qb * QB
        yn_ms = emit_norm_qb(P, z, z.X, lambda fci: qsl(z.xnq, fci), t0, kind="ln")
        k_ms = None
        for h in range(HEADS):
            ch, sl = h // 2, (h % 2) * 64
            bank = h % 2
            mm = P.op("tensor", lambda e, ch=ch, sl=sl, bank=bank:
                      e.matmul(z.pb[bank][sl:sl + 64, :], z.wkT[sl:sl + 64, 0:64],
                               qsl(z.xnq, ch)[sl:sl + 64, :], start=True, stop=True),
                      waits=[wk_ms, yn_ms, z.bank_free[bank]], inc="PE")
            k_ms = P.op("vector", lambda e, ch=ch, sl=sl, bank=bank, t0=t0:
                        e.tensor_copy(xsl(z.kb, ch, t0, QB)[sl:sl + 64, :],
                                      z.pb[bank][sl:sl + 64, :]),
                        waits=[mm], inc="DV")
            z.bank_free[bank] = k_ms
        for tt in range(4):
            kc_abs = qb * 4 + tt
            for h in range(HEADS):
                ch, sl = h // 2, (h % 2) * 64
                bank = 2 + h % 2
                mm = P.op("tensor", lambda e, ch=ch, sl=sl, tt=tt, bank=bank:
                          e.matmul(z.pb[bank][:, 0:64],
                                   qsl(z.xnq, ch)[sl:sl + 64, tt * 128:tt * 128 + 128],
                                   z.wvT[sl:sl + 64, 0:64], start=True, stop=True),
                          waits=[wv_ms, yn_ms, ones_ms, z.bank_free[bank]], inc="PE")
                dst = z.vb[:, (kc_abs * HEADS + h) * 65:(kc_abs * HEADS + h) * 65 + 64]
                v_ms = P.op("vector", lambda e, dst=dst, bank=bank:
                            e.tensor_copy(dst, z.pb[bank][:, 0:64]),
                            waits=[mm], inc="DV")
                z.bank_free[bank] = v_ms
        mk_ms = emit_sqmax(P, z, lambda i, t0=t0: xsl(z.kb, i, t0, QB), FC,
                           z.seg64b, 8, init=(qb == 0), ready=[P.now("DV")])
        mms = []
        for h in range(HEADS):
            ch, sl = h // 2, (h % 2) * 64
            mm = P.op("tensor", lambda e, ch=ch, sl=sl:
                      e.matmul(z.pb[ch][sl:sl + 64, :], z.wqT[sl:sl + 64, 0:64],
                               qsl(z.xnq, ch)[sl:sl + 64, :], start=True, stop=True),
                      waits=[wq_ms, yn_ms, z.bank_free[ch]], inc="PE")
            mms.append(mm)
        qev = None
        for h in range(HEADS):
            ch, sl = h // 2, (h % 2) * 64
            qev = P.op("vector", lambda e, ch=ch, sl=sl:
                       e.tensor_copy(qsl(z.xnq, ch)[sl:sl + 64, :],
                                     z.pb[ch][sl:sl + 64, :]),
                       waits=[mms[-1]], inc="DV")
            z.bank_free[ch] = qev
        mq_ms = emit_sqmax(P, z, lambda i: qsl(z.xnq, i), FC, z.seg64b, 12, init=True,
                           ready=[P.now("DV")])
        bias_ms, bias = emit_bias_from_sqmax(P, z, mk_ms, mq_ms)
        emit_attn_qb(P, z, qb=qb, q_src=lambda ch: qsl(z.xnq, ch),
                     kb=z.kb, vb=z.vb, bias=bias, bias_ms=bias_ms,
                     q_ready=P.now("DV"), kv_ready=P.now("DV"),
                     out_target=z.X, wout=z.wD, wout_ready=wo_ms, scale=None)


def emit_attn0(P, z, w, *, first_seq):
    wq_ms = emit_weight_load(P, z.wQ0, w["wq"], D, 256, waits=z.pool_free)
    wk_ms = emit_weight_load(P, z.wK0, w["wk"], D, 256, waits=z.pool_free)
    wv_ms = emit_weight_load(P, z.wV0, w["wv"], D, D, waits=z.pool_free)
    emit_ln_rows(P, z, z.X, rms=False, eps="eps_ln")
    ones_ms = P.op("gpsimd", lambda e: e.memset(
        z.vb.rearrange("p (k e) -> p k e", e=65)[:, :, 64:65], 1.0),
        waits=z.pool_free_g, inc="PL")
    zm = None
    if first_seq:
        zm = P.op("gpsimd", lambda e: e.memset(z.kb16.bitcast(F32), 0.0),
                  waits=z.pool_free_g, inc="PL")

    def l2norm_16(buf_slicer, ready):
        sq = P.op("scalar", lambda e: e.activation(z.SCB[:, 0:QB], buf_slicer(), AF.Square),
                  waits=[ready, z.ms_scb_free], inc="AC")
        mm = P.op("tensor", lambda e: e.matmul(
            z.pb[6][0:2, :], z.segqb, z.SCB[:, 0:QB], start=True, stop=True),
            waits=[sq, z.bank_free[6]], inc="PE")
        z.ms_scb_free = mm
        l1 = P.op("scalar", lambda e: e.activation(
            z.ROWS2[0:2, 512:512 + QB], z.pb[6][0:2, :], AF.Ln, bias=z.cst1("zero", rows=2)),
            waits=[mm, z.ms_rowtemp], inc="AC")
        z.bank_free[6] = l1
        l2 = P.op("scalar", lambda e: e.activation(
            z.ROWS2[0:2, 512:512 + QB], z.ROWS2[0:2, 512:512 + QB], AF.Exp, scale=0.5),
            waits=[l1], inc="AC")
        mx = P.op("vector", lambda e: e.tensor_scalar(
            z.ROWS2[0:2, 512:512 + QB], z.ROWS2[0:2, 512:512 + QB], 1e-12, None, OP.max),
            waits=[l2], inc="DV")
        rc = P.op("vector", lambda e: e.reciprocal(
            z.ROWS2[0:2, 512:512 + QB], z.ROWS2[0:2, 512:512 + QB]),
            waits=[mx], inc="DV")
        z.ms_rowtemp = rc
        ex = P.op("tensor", lambda e: e.matmul(
            z.pb[6][:, :], z.exq, z.ROWS2[0:2, 512:512 + QB], start=True, stop=True),
            waits=[rc, z.bank_free[6]], inc="PE")
        tt = P.op("vector", lambda e: e.tensor_tensor(
            buf_slicer(), buf_slicer(), z.pb[6][:, :], OP.mult),
            waits=[ex], inc="DV")
        z.bank_free[6] = tt
        return tt

    for qb in range(NQB):
        t0 = qb * QB
        yn_ms = emit_norm_qb(P, z, z.X, lambda fci: qsl(z.xnq, fci), t0, kind="ln")
        for h in range(HEADS):
            g, sl = h // 2, (h % 2) * 64
            bank = h % 2
            mm = None
            for kc in range(FC):
                mm = P.op("tensor", lambda e, h=h, kc=kc, sl=sl, bank=bank:
                          e.matmul(z.pb[bank][sl:sl + 16, :],
                                   z.wK0[:, kc * 256 + h * 16: kc * 256 + h * 16 + 16],
                                   qsl(z.xnq, kc), start=(kc == 0), stop=(kc == FC - 1)),
                          waits=([wk_ms, yn_ms, zm, z.bank_free[bank]] if kc == 0 else ()),
                          inc=("PE" if kc == FC - 1 else None))
            kev = P.op("vector", lambda e, g=g, sl=sl, bank=bank, t0=t0:
                       e.tensor_copy(z.kb16[sl:sl + 16, g * S + t0: g * S + t0 + QB],
                                     z.pb[bank][sl:sl + 16, :]),
                       waits=[mm], inc="DV")
            z.bank_free[bank] = kev
        for g in range(FC):
            l2norm_16(lambda g=g, t0=t0: z.kb16[:, g * S + t0: g * S + t0 + QB],
                      P.now("DV"))
        for tti in range(4):
            kc_abs = qb * 4 + tti
            for vh in range(2):
                bank = 2 + vh
                mm = None
                for kc in range(FC):
                    mm = P.op("tensor", lambda e, kc=kc, tti=tti, vh=vh, bank=bank:
                              e.matmul(z.pb[bank][:, :],
                                       qsl(z.xnq, kc)[:, tti * 128:tti * 128 + 128],
                                       z.wV0[:, kc * D + vh * 512: kc * D + vh * 512 + 512],
                                       start=(kc == 0), stop=(kc == FC - 1)),
                              waits=([wv_ms, yn_ms, ones_ms, z.bank_free[bank]]
                                     if kc == 0 else ()),
                              inc=("PE" if kc == FC - 1 else None))
                sq = P.op("scalar", lambda e, bank=bank:
                          e.activation(z.SCB[:, 0:QB], z.pb[bank][:, :], AF.Square),
                          waits=[mm, z.ms_scb_free], inc="AC")
                red = None
                for hh in range(8):
                    red = P.op("vector", lambda e, hh=hh: e.tensor_reduce(
                        z.GT[:, hh:hh + 1], z.SCB[:, hh * 64:hh * 64 + 64],
                        axis=AX.X, op=OP.add),
                        waits=[sq, z.ms_gt_free], inc="DV")
                z.ms_scb_free = red
                l1 = P.op("scalar", lambda e: e.activation(
                    z.GT[:, 8:16], z.GT[:, 0:8], AF.Ln, bias=z.cst1("zero")),
                    waits=[red], inc="AC")
                l2 = P.op("scalar", lambda e: e.activation(
                    z.GT[:, 8:16], z.GT[:, 8:16], AF.Exp, scale=0.5),
                    waits=[l1], inc="AC")
                mx = P.op("vector", lambda e: e.tensor_scalar(
                    z.GT[:, 8:16], z.GT[:, 8:16], 1e-12, None, OP.max),
                    waits=[l2], inc="DV")
                rc = P.op("vector", lambda e: e.reciprocal(z.GT[:, 8:16], z.GT[:, 8:16]),
                          waits=[mx], inc="DV")
                ev = None
                for hh in range(8):
                    h = vh * 8 + hh
                    dst = z.vb[:, (kc_abs * HEADS + h) * 65:(kc_abs * HEADS + h) * 65 + 64]
                    ev = P.op("vector", lambda e, dst=dst, hh=hh, bank=bank:
                              e.tensor_scalar(dst, z.pb[bank][:, hh * 64:hh * 64 + 64],
                                              z.GT[:, 8 + hh:9 + hh], None, OP.mult),
                              waits=[rc], inc="DV")
                z.ms_gt_free = ev
                z.bank_free[bank] = ev
    wo_ms = emit_weight_load(P, z.wD, w["wout"], D, D, waits=[P.now("PE")] + list(z.pool_free))
    for qb in range(NQB):
        t0 = qb * QB
        yn_ms = emit_norm_qb(P, z, z.X, lambda fci: qsl(z.xnq, fci), t0, kind="ln")
        mms = []
        for h in range(HEADS):
            g, sl = h // 2, (h % 2) * 64
            mm = None
            for kc in range(FC):
                mm = P.op("tensor", lambda e, h=h, kc=kc, g=g, sl=sl:
                          e.matmul(z.pb[g][sl:sl + 16, :],
                                   z.wQ0[:, kc * 256 + h * 16: kc * 256 + h * 16 + 16],
                                   qsl(z.xnq, kc), start=(kc == 0), stop=(kc == FC - 1)),
                          waits=([wq_ms, yn_ms, z.bank_free[g]] if kc == 0 else ()),
                          inc=("PE" if kc == FC - 1 else None))
            mms.append(mm)
        qev = None
        for h in range(HEADS):
            g, sl = h // 2, (h % 2) * 64
            qev = P.op("vector", lambda e, g=g, sl=sl:
                       e.tensor_copy(qsl(z.xnq, g)[sl:sl + 16, :],
                                     z.pb[g][sl:sl + 16, :]),
                       waits=[mms[-1]], inc="DV")
            z.bank_free[g] = qev
        for g in range(FC):
            qev = l2norm_16(lambda g=g: qsl(z.xnq, g), qev)
        q_ready = P.now("DV")
        for h in range(HEADS):
            g, sl = h // 2, (h % 2) * 64
            for kc in range(16):
                sbank = kc % 2
                smm = P.op("tensor", lambda e, kc=kc, g=g, sl=sl, sbank=sbank:
                           e.matmul(z.pb[sbank][:, :],
                                    z.kb16[sl:sl + 16, g * S + kc * 128: g * S + kc * 128 + 128],
                                    qsl(z.xnq, g)[sl:sl + 16, :],
                                    start=True, stop=True),
                           waits=[q_ready, z.bank_free[sbank]], inc="PE")
                rl = P.op("scalar", lambda e, sbank=sbank:
                          e.activation(z.SCR, z.pb[sbank][:, :], AF.Relu),
                          waits=[smm, z.ms_scr_free], inc="AC")
                z.bank_free[sbank] = rl
                pt = z.ptb[:, (kc % 2) * QB:(kc % 2) * QB + QB]
                sqr = P.op("vector", lambda e, pt=pt:
                           e.tensor_tensor(pt, z.SCR, z.SCR, OP.mult),
                           waits=[rl, z.pt_free[kc % 2]], inc="DV")
                z.ms_scr_free = sqr
                av_waits = [sqr] + ([z.bank_free[2]] if kc == 0 else [])
                avm = P.op("tensor", lambda e, kc=kc, h=h, pt=pt:
                           e.matmul(z.pb[2][0:64, :],
                                    z.vb[:, (kc * HEADS + h) * 65:(kc * HEADS + h) * 65 + 64],
                                    pt, start=(kc == 0), stop=(kc == 15)),
                           waits=av_waits, inc="PE")
                z.pt_free[kc % 2] = avm
            ao_last = P.op("vector", lambda e, g=g, sl=sl:
                           e.tensor_copy(z.ao[sl:sl + 64, g * QB:(g + 1) * QB],
                                         z.pb[2][0:64, :]),
                           waits=[avm], inc="DV")
            z.bank_free[2] = ao_last
        tn_last = None
        for fci in range(FC):
            sq = P.op("scalar", lambda e, fci=fci:
                      e.activation(z.SCB[:, 0:QB], z.ao[:, fci * QB:(fci + 1) * QB], AF.Square),
                      waits=[ao_last, z.ms_scb_free], inc="AC")
            mm = P.op("tensor", lambda e: e.matmul(
                z.pb[6][0:2, :], z.seg64b, z.SCB[:, 0:QB], start=True, stop=True),
                waits=[sq, z.bank_free[6]], inc="PE")
            z.ms_scb_free = mm
            l1 = P.op("scalar", lambda e: e.activation(
                z.ROWS2[0:2, 512:512 + QB], z.pb[6][0:2, :], AF.Ln, bias=z.cst1("zero", rows=2)),
                waits=[mm, z.ms_rowtemp], inc="AC")
            z.bank_free[6] = l1
            n1 = P.op("scalar", lambda e: e.activation(
                z.ROWS2[0:2, 512:512 + QB], z.ROWS2[0:2, 512:512 + QB], AF.Exp, scale=0.5),
                waits=[l1], inc="AC")
            th = P.op("scalar", lambda e: e.activation(
                z.ROWS2[0:2, 0:QB], z.ROWS2[0:2, 512:512 + QB], AF.Tanh),
                waits=[n1], inc="AC")
            mx = P.op("vector", lambda e: e.tensor_scalar(
                z.ROWS2[0:2, 512:512 + QB], z.ROWS2[0:2, 512:512 + QB], 1e-12, None, OP.max),
                waits=[n1, th], inc="DV")
            rc = P.op("vector", lambda e: e.reciprocal(
                z.ROWS2[0:2, 512:512 + QB], z.ROWS2[0:2, 512:512 + QB]),
                waits=[mx], inc="DV")
            ff = P.op("vector", lambda e: e.tensor_tensor(
                z.ROWS2[0:2, 0:QB], z.ROWS2[0:2, 0:QB], z.ROWS2[0:2, 512:512 + QB], OP.mult),
                waits=[rc], inc="DV")
            z.ms_rowtemp = ff
            ex = P.op("tensor", lambda e: e.matmul(
                z.pb[6][:, :], z.ex2, z.ROWS2[0:2, 0:QB], start=True, stop=True),
                waits=[ff, z.bank_free[6]], inc="PE")
            tn_last = P.op("vector", lambda e, fci=fci: e.tensor_tensor(
                z.ao[:, fci * QB:(fci + 1) * QB], z.ao[:, fci * QB:(fci + 1) * QB],
                z.pb[6][:, :], OP.mult),
                waits=[ex], inc="DV")
            z.bank_free[6] = tn_last
        for fci in range(FC):
            bank = 4 + fci % 2
            waits = [wo_ms, tn_last, z.bank_free[bank]]
            mm = None
            for c in range(FC):
                mm = P.op("tensor", lambda e, c=c, fci=fci, bank=bank:
                          e.matmul(z.pb[bank][:, :],
                                   z.wD[:, c * D + fci * 128: c * D + fci * 128 + 128],
                                   z.ao[:, c * QB:(c + 1) * QB],
                                   start=(c == 0), stop=(c == FC - 1)),
                          waits=(waits if c == 0 else ()), inc=("PE" if c == FC - 1 else None))
            ev = P.op("vector", lambda e, fci=fci, qb=qb, bank=bank:
                      e.tensor_tensor(xsl(z.X, fci, qb * QB, QB),
                                      z.pb[bank][:, :], xsl(z.X, fci, qb * QB, QB), OP.add),
                      waits=[mm], inc="DV")
            z.bank_free[bank] = ev


def emit_f0(P, z, w):
    emit_ln_rows(P, z, z.X, rms=False, eps="eps_ln")
    rw = [P.now("DV"), P.now("AC")]
    h_ms = None
    for qb in range(NQB):
        h_ms = emit_norm_qb(P, z, z.X, lambda fci, qb=qb: xsl(z.H, fci, qb * QB, QB),
                            qb * QB, kind="ln", row_waits=rw)
    park_ms = None
    pg = P.group("gpsimd")
    for fci in range(FC):
        park_ms = P.op("gpsimd", lambda e, fci=fci:
                       e.dma_start(out=P.xpark[fci * 128:(fci + 1) * 128, :],
                                   in_=xsl(z.X, fci, 0, S)),
                       waits=[h_ms], inc=pg, inc_amt=16)
    prev_free = z.pool_free
    z.pool_free = [park_ms] + list(prev_free)
    z.pool_free_g = [park_ms]
    carve_attn(z, "mla", "x")
    emit_ln_rows(P, z, z.H, rms=True, eps="eps_rms")
    emit_mla(P, z, w)
    mla_done = [P.now("DV"), P.now("AC")]
    if DBG["mode"] != "mla":
        carve_moe(z, "x")
        z.pool_free = mla_done
        z.pool_free_g = mla_done
        z.hid_guard = None
        emit_ln_rows(P, z, z.H, rms=True, eps="eps_rms")
        emit_moe(P, z, w, variant="l0", target=z.H)
    moe_done = [P.now("DV"), P.now("AC"), P.now("PE")]
    if DBG["mode"] == "mla-stop":
        return None
    rest_ms = None
    rg = P.group()
    for fci in range(FC):
        rest_ms = P.op("sync", lambda e, fci=fci:
                       e.dma_start(out=xsl(z.X, fci, 0, S),
                                   in_=P.xpark[fci * 128:(fci + 1) * 128, :]),
                       waits=moe_done + [park_ms], inc=rg, inc_amt=16)
    last = None
    for fci in range(FC):
        last = P.op("vector", lambda e, fci=fci:
                    e.scalar_tensor_tensor(xsl(z.X, fci, 0, S), xsl(z.H, fci, 0, S),
                                           0.5, xsl(z.X, fci, 0, S), OP.mult, OP.add),
                    waits=[rest_ms] + moe_done, inc="DV")
    return last


def build_phase(kind):
    nc = bass.Bass("TRN2", target_bir_lowering=False)
    x_dram = nc.dram_tensor("x", [NSEQ, D, S], F32, kind="ExternalInput").ap()
    y_dram = nc.dram_tensor("y", [NSEQ, D, S], F32, kind="ExternalOutput").ap()
    cf_dram = nc.dram_tensor("cf", [128, CF_COLS], F32, kind="ExternalInput").ap()
    cb_dram = nc.dram_tensor("cb", [128, CB_COLS], BF16, kind="ExternalInput").ap()
    wd = {}

    def win(name, r, c):
        wd[name] = nc.dram_tensor(name, [r, c], BF16, kind="ExternalInput").ap()
        return wd[name]

    if kind == "f0":
        w = dict(wq=win("cv_l0f_wq", D, D), wdkv=win("cv_l0f_wdkv", D, LAT),
                 wuk=win("cv_l0f_wuk", LAT, D), wuv=win("cv_l0f_wuv", LAT, D),
                 wo=win("cv_l0f_wo", D, D), gate=win("cv_l0f_gate", D, 8),
                 w1=win("cv_l0f_w1", EXP * D, EDIM), w2=win("cv_l0f_w2", EXP * EDIM, D))
    elif kind == "g1":
        w = dict(gate=win("cv_l1f_wg", D, 8),
                 w1=win("cv_l1f_w1", EXP * D, EDIM), w2=win("cv_l1f_w2", EXP * EDIM, D))
    elif kind == "a0":
        w = dict(wq=win("cv_l0a_wq", D, 256), wk=win("cv_l0a_wk", D, 256),
                 wv=win("cv_l0a_wv", D, D), wout=win("cv_l0a_wout", D, D))
    elif kind == "a1":
        w = dict(wqT=win("cv_l1a_wqT", 64, 64), wkT=win("cv_l1a_wkT", 64, 64),
                 wvT=win("cv_l1a_wvT", 64, 64), woT=win("cv_l1a_woT", D, D))

    aux = {}
    if kind in ("f0", "g1"):
        aux["sel8f"] = nc.dram_tensor("sel8f", [8, 1024], BF16, kind="ExternalInput").ap()
    if kind in ("f0", "a1"):
        aux["cmaskb"] = nc.dram_tensor("cmaskb", [128, 2048], BF16, kind="ExternalInput").ap()
    if kind == "f0":
        aux["g1c"] = nc.dram_tensor("g1c", [128, 8], F32, kind="ExternalInput").ap()
        aux["g2c"] = nc.dram_tensor("g2c", [128, 8], F32, kind="ExternalInput").ap()
    if kind == "g1":
        aux["b1f"] = nc.dram_tensor("b1f", [128, 128], F32, kind="ExternalInput").ap()
        aux["b2f"] = nc.dram_tensor("b2f", [128, 64], F32, kind="ExternalInput").ap()

    with ExitStack() as ctx:
        z = carve(nc, ctx)
        P = Prog(nc, ctx)
        if kind == "f0":
            P.xpark = nc.dram_tensor("xpark", [D, S], F32, kind="Internal").ap()
        if kind in ("f0", "g1"):
            P.sel8_dram = aux["sel8f"]
        if kind in ("f0", "a1"):
            P.cmask_dram = aux["cmaskb"]
        block = ctx.enter_context(nc.Block())
        cg = P.group()
        cms = [P.op("sync", lambda e: e.dma_start(out=z.CSTF, in_=cf_dram), inc=cg, inc_amt=16),
               P.op("sync", lambda e: e.dma_start(out=z.CSTB, in_=cb_dram), inc=cg, inc_amt=16)]
        if kind == "f0":
            cms.append(P.op("sync", lambda e: e.dma_start(out=z.g1c, in_=aux["g1c"]),
                            inc=cg, inc_amt=16))
            cms.append(P.op("sync", lambda e: e.dma_start(out=z.g2c, in_=aux["g2c"]),
                            inc=cg, inc_amt=16))
        if kind == "g1":
            cms.append(P.op("sync", lambda e: e.dma_start(out=z.b1f, in_=aux["b1f"]),
                            inc=cg, inc_amt=16))
            cms.append(P.op("sync", lambda e: e.dma_start(out=z.b2f, in_=aux["b2f"]),
                            inc=cg, inc_amt=16))
        x_free = None
        st_groups = []
        for s in range(DBG["nseq"] or NSEQ):
            z.pool_free = [P.now("DV"), P.now("AC")] if s > 0 else [cms[-1]]
            z.pool_free_g = [x_free] if x_free else []
            ld = None
            lg = P.group()
            for fci in range(FC):
                ld = P.op("sync", lambda e, fci=fci, s=s:
                          e.dma_start(out=xsl(z.X, fci, 0, S),
                                      in_=x_dram[s, fci * 128:(fci + 1) * 128, :]),
                          waits=[x_free], inc=lg, inc_amt=16)
            P.op("vector", lambda e: e.nop(), waits=[ld] + cms, inc="DV")
            if kind == "f0":
                emit_f0(P, z, w)
            elif kind == "g1":
                carve_moe(z, "h")
                z.hid_guard = None
                if DBG["mode"] == "id":
                    pass
                elif DBG["mode"] == "ln":
                    emit_ln_rows(P, z, z.X, rms=False, eps="eps_ln")
                    rw = [P.now("DV"), P.now("AC")]
                    for qb in range(NQB):
                        emit_norm_qb(P, z, z.X,
                                     lambda fci, qb=qb: xsl(z.X, fci, qb * QB, QB),
                                     qb * QB, kind="ln", row_waits=rw)
                else:
                    emit_ln_rows(P, z, z.X, rms=False, eps="eps_ln")
                    emit_moe(P, z, w, variant="l1", target=z.X)
            elif kind == "a0":
                carve_attn(z, "a0", "h")
                emit_attn0(P, z, w, first_seq=(s == 0))
            elif kind == "a1":
                carve_attn(z, "a1", "h")
                emit_attn1(P, z, w)
            allms = [P.now("DV"), P.now("AC"), P.now("PE")]
            sg = P.group("gpsimd")
            for fci in range(FC):
                x_free = P.op("gpsimd", lambda e, fci=fci, s=s:
                              e.dma_start(out=y_dram[s, fci * 128:(fci + 1) * 128, :],
                                          in_=xsl(z.X, fci, 0, S)),
                              waits=allms, inc=sg, inc_amt=16)
            st_groups.append(x_free)
        P.op("gpsimd", lambda e: e.nop(), waits=st_groups)
        P.play(block)
    return nc


_cache = {}


def _get(name, builder):
    if name not in _cache:
        _cache[name] = builder()
    return _cache[name]


def _conv_inputs(inputs):
    tr = np.ascontiguousarray
    kcols = (np.arange(HEADS)[:, None] * 80 + np.arange(16)[None, :]).reshape(-1)
    vcols = (np.arange(HEADS)[:, None] * 80 + 16 + np.arange(64)[None, :]).reshape(-1)
    wkv = np.asarray(inputs["L0_attn_Wkv"], np.float32)
    d = {
        "l0a_wq": inputs["L0_attn_Wq"],
        "l0a_wk": tr(wkv[:, kcols]),
        "l0a_wv": tr(wkv[:, vcols]),
        "l0a_wout": inputs["L0_attn_Wout"],
        "l0f_wq": inputs["L0_ffn_Wq"],
        "l0f_wdkv": inputs["L0_ffn_Wdkv"],
        "l0f_wuk": inputs["L0_ffn_Wuk"],
        "l0f_wuv": inputs["L0_ffn_Wuv"],
        "l0f_wo": inputs["L0_ffn_Wo"],
        "l0f_gate": inputs["L0_ffn_gate_w"],
        "l0f_w1": np.asarray(inputs["L0_ffn_w1"]).reshape(EXP * D, EDIM),
        "l0f_w2": np.asarray(inputs["L0_ffn_w2"]).reshape(EXP * EDIM, D),
        "l1a_wqT": tr(np.asarray(inputs["L1_attn_Wq"]).T),
        "l1a_wkT": tr(np.asarray(inputs["L1_attn_Wk"]).T),
        "l1a_wvT": tr(np.asarray(inputs["L1_attn_Wv"]).T),
        "l1a_woT": tr(np.asarray(inputs["L1_attn_Wo"]).T),
        "l1f_wg": inputs["L1_ffn_Wg"],
        "l1f_w1": np.asarray(inputs["L1_ffn_W1"]).reshape(EXP * D, EDIM),
        "l1f_w2": np.asarray(inputs["L1_ffn_W2"]).reshape(EXP * EDIM, D),
    }
    return {k: np.ascontiguousarray(v, dtype=np.float32) for k, v in d.items()}


def kernel(**inputs):
    cin = _conv_inputs(inputs)
    nc_conv = _get("conv", build_conv)
    conv_in = {name: cin[name] for name, _, _ in CONV_SPECS}
    res = run_bass_kernel_spmd(nc_conv, [conv_in] * NCORES, core_ids=list(range(NCORES)))
    cw = dict(res.results[0])

    x = np.asarray(inputs["L0_attn_tokens"], dtype=np.float32)
    xc_ = x.reshape(NCORES, NSEQ, S, D).transpose(0, 1, 3, 2)
    xs = [np.ascontiguousarray(xc_[i]) for i in range(NCORES)]
    g1c = np.ascontiguousarray(np.asarray(inputs["L0_ffn_g1"], np.float32).reshape(FC, 128).T)
    g2c = np.ascontiguousarray(np.asarray(inputs["L0_ffn_g2"], np.float32).reshape(FC, 128).T)
    b1 = np.asarray(inputs["L1_ffn_b1"], np.float32)
    b2 = np.asarray(inputs["L1_ffn_b2"], np.float32)
    b1f = np.ascontiguousarray(b1.reshape(EXP, 16, 128).transpose(2, 0, 1).reshape(128, 128))
    b2f = np.ascontiguousarray(b2.reshape(EXP, 8, 128).transpose(2, 0, 1).reshape(128, 64))

    def run(kind, xs_cur, extra):
        nc = _get(kind, lambda: build_phase(kind))
        maps = []
        for i in range(NCORES):
            m = {"x": xs_cur[i], "cf": CF, "cb": CB}
            m.update(extra)
            maps.append(m)
        r = run_bass_kernel_spmd(nc, maps, core_ids=list(range(NCORES)))
        return [r.results[i]["y"] for i in range(NCORES)]

    f0_w = {k: cw[k] for k in ("cv_l0f_wq", "cv_l0f_wdkv", "cv_l0f_wuk", "cv_l0f_wuv",
                               "cv_l0f_wo", "cv_l0f_gate", "cv_l0f_w1", "cv_l0f_w2")}
    f0_w.update({"g1c": g1c, "g2c": g2c, "sel8f": SEL8H, "cmaskb": CMASKH})
    a0_w = {k: cw[k] for k in ("cv_l0a_wq", "cv_l0a_wk", "cv_l0a_wv", "cv_l0a_wout")}
    g1_w = {k: cw[k] for k in ("cv_l1f_wg", "cv_l1f_w1", "cv_l1f_w2")}
    g1_w.update({"b1f": b1f, "b2f": b2f, "sel8f": SEL8H})
    a1_w = {k: cw[k] for k in ("cv_l1a_wqT", "cv_l1a_wkT", "cv_l1a_wvT", "cv_l1a_woT")}
    a1_w["cmaskb"] = CMASKH

    xs = run("f0", xs, f0_w)
    xs = run("a0", xs, a0_w)
    xs = run("f0", xs, f0_w)
    xs = run("g1", xs, g1_w)
    xs = run("a1", xs, a1_w)
    xs = run("g1", xs, g1_w)

    out = np.stack(xs).reshape(NCORES, NSEQ, D, S).transpose(0, 1, 3, 2)
    return np.ascontiguousarray(out.reshape(B, S, D), dtype=np.float32)


# version 02 (final)
# speedup vs baseline: 1.1809x; 1.1809x over previous
import numpy as np
import ml_dtypes
from contextlib import ExitStack
import concourse.bass as bass
import concourse.mybir as mybir
from concourse.bass_utils import run_bass_kernel_spmd

F32 = mybir.dt.float32
BF16 = mybir.dt.bfloat16
AF = mybir.ActivationFunctionType
OP = mybir.AluOpType
AX = mybir.AxisListType
BF = ml_dtypes.bfloat16

NCORES = 8
B, S, D = 32, 2048, 1024
NSEQ = B // NCORES
FC = D // 128
HEADS = 16
EXP = 8
EDIM = 2048
LAT = 512
QB = 512
NQB = S // QB

ENGS = ("sync", "gpsimd", "tensor", "vector", "scalar")


class Prog:

    def __init__(self, nc, ctx):
        self.nc = nc
        self.streams = {e: [] for e in ENGS}
        self.sems = {}
        self.cnt = {}
        self.watermark = {e: {} for e in ENGS}
        for name in ("PE", "DV", "AC", "PL"):
            self.sems[name] = ctx.enter_context(nc.semaphore(name))
            self.cnt[name] = 0
        self.ndm = 14
        for i in range(self.ndm):
            self.sems[f"DM{i}"] = ctx.enter_context(nc.semaphore(f"DM{i}"))
            self.cnt[f"DM{i}"] = 0
        self.dma_rot = 0
        self.ngp = 40
        for i in range(self.ngp):
            self.sems[f"GP{i}"] = ctx.enter_context(nc.semaphore(f"GP{i}"))
            self.cnt[f"GP{i}"] = 0
        self.gp_next = 0
        self.bank_free = [None] * 8
        self.ms_scr_free = None
        self.ms_scb_free = None
        self.ms_scb2_free = None
        self.ms_scb_sq = None
        self.ms_gt_free = None
        self.ms_gt_ac = None
        self.ms_glq_free = None
        self.ms_cbs_free = None
        self.ms_rowtemp = None
        self.pt_free = [None, None]
        self.hid_guard = None
        self.pool_free = []
        self.pool_free_g = []
        self.cmask_ms = None

    def group(self, queue="sync"):
        if queue == "gpsimd":
            g = f"GP{self.gp_next}"
            self.gp_next += 1
            assert self.gp_next <= self.ngp, "out of gpsimd dma semaphores"
            return g
        g = f"DM{self.dma_rot % self.ndm}"
        self.dma_rot += 1
        return g

    OWN_SEM = {"vector": "DV", "scalar": "AC", "gpsimd": "PL"}

    def op(self, eng, fn, waits=(), inc=None, inc_amt=1):
        own = self.OWN_SEM.get(eng)
        waits = list(waits)
        if own is not None and self.cnt[own] > 0:
            waits.append((own, self.cnt[own]))
        byname = {}
        for w in waits:
            if w is None:
                continue
            sem, val = w
            if val > byname.get(sem, 0):
                byname[sem] = val
        wl = []
        wm = self.watermark[eng]
        for sem, val in byname.items():
            if val > wm.get(sem, 0):
                wm[sem] = val
                wl.append((sem, val))
        ms = None
        if inc is not None:
            self.cnt[inc] += inc_amt
            ms = (inc, self.cnt[inc])
        self.streams[eng].append((tuple(wl), fn, inc, inc_amt))
        return ms

    def now(self, sem):
        return (sem, self.cnt[sem])

    def play(self, block):
        sems = self.sems
        for engname in ENGS:
            stream = self.streams[engname]
            if not stream:
                continue

            def make(stream):
                def run(e):
                    for waits, fn, inc, inc_amt in stream:
                        for sem, val in waits:
                            e.wait_ge(sems[sem], val)
                        ins = fn(e)
                        if inc is not None:
                            ins.then_inc(sems[inc], inc_amt)
                return run
            getattr(block, engname)(make(stream))


class Z:
    pass


CF_COLS = 768
CB_COLS = 8


def host_consts():
    cf = np.zeros((128, CF_COLS), np.float32)
    col = {}
    c = 0

    def take(n, name):
        nonlocal c
        col[name] = c
        c += n
    take(1, "ones"); cf[:, col["ones"]] = 1.0
    take(1, "eps_ln"); cf[:, col["eps_ln"]] = 1e-5
    take(1, "eps_rms"); cf[:, col["eps_rms"]] = 1e-6
    take(1, "eps_l2"); cf[:, col["eps_l2"]] = 1e-12
    take(1, "zero")
    take(128, "ones_row"); cf[:, col["ones_row"]:col["ones_row"] + 128] = 1.0
    take(2, "seg64")
    cf[0:64, col["seg64"]] = 1.0
    cf[64:128, col["seg64"] + 1] = 1.0
    take(128, "ex2")
    cf[0, col["ex2"]:col["ex2"] + 64] = 1.0
    cf[1, col["ex2"] + 64:col["ex2"] + 128] = 1.0
    take(2, "segq")
    cf[0:16, col["segq"]] = 1.0
    cf[64:80, col["segq"] + 1] = 1.0
    take(128, "exq")
    cf[0, col["exq"]:col["exq"] + 16] = 1.0
    cf[1, col["exq"] + 64:col["exq"] + 80] = 1.0
    take(128, "ident")
    cf[:, col["ident"]:col["ident"] + 128] = np.eye(128, dtype=np.float32)
    assert c <= CF_COLS, c

    cb = np.zeros((128, CB_COLS), BF)
    colb = {}
    colb["seg64b"] = 0
    cb[0:64, 0] = BF(1.0)
    cb[64:128, 1] = BF(1.0)
    colb["segqb"] = 2
    cb[0:16, 2] = BF(1.0)
    cb[64:80, 3] = BF(1.0)
    colb["onesb"] = 4
    cb[:, 4] = BF(1.0)
    sel = np.zeros((8, 1024), BF)
    for e in range(8):
        sel[e, e * 128:(e + 1) * 128] = BF(1.0)
    cmask = np.zeros((128, 2048), BF)
    kk = np.arange(128)[:, None]
    qq = np.arange(512)[None, :]
    for ri, r in enumerate((0, 128, 256, 384)):
        cmask[:, ri * 512:(ri + 1) * 512] = (qq - kk - r >= 0).astype(BF)
    return cf, col, cb, colb, sel, cmask


CF, CFCOL, CB, CBCOL, SEL8H, CMASKH = host_consts()


CONV_SPECS = [
    ("l0a_wq", D, 256), ("l0a_wk", D, 256), ("l0a_wv", D, D), ("l0a_wout", D, D),
    ("l0f_wq", D, D), ("l0f_wdkv", D, LAT), ("l0f_wuk", LAT, D),
    ("l0f_wuv", LAT, D), ("l0f_wo", D, D), ("l0f_gate", D, 8),
    ("l0f_w1", EXP * D, EDIM), ("l0f_w2", EXP * EDIM, D),
    ("l1a_wqT", 64, 64), ("l1a_wkT", 64, 64), ("l1a_wvT", 64, 64),
    ("l1a_woT", D, D), ("l1f_wg", D, 8),
    ("l1f_w1", EXP * D, EDIM), ("l1f_w2", EXP * EDIM, D),
]


def build_conv():
    nc = bass.Bass("TRN2", target_bir_lowering=False)
    ins, outs = {}, {}
    for name, r, c in CONV_SPECS:
        ins[name] = nc.dram_tensor(name, [r, c], F32, kind="ExternalInput").ap()
        outs[name] = nc.dram_tensor("cv_" + name, [r, c], BF16, kind="ExternalOutput").ap()
    with ExitStack() as ctx:
        stg = [ctx.enter_context(nc.sbuf_tensor(f"stg{i}", [128, 4096], F32)) for i in range(2)]
        ob = [ctx.enter_context(nc.sbuf_tensor(f"ob{i}", [128, 4096], BF16)) for i in range(2)]
        P = Prog(nc, ctx)
        block = ctx.enter_context(nc.Block())
        free_ms = [None, None]
        st_ms = [None, None]
        slot = 0
        for name, r, c in CONV_SPECS:
            nchunk = max(1, r // 128)
            rows = min(r, 128)
            for i in range(nchunk):
                for j0 in range(0, c, 4096):
                    w = min(4096, c - j0)
                    sl = slot % 2
                    slot += 1
                    if r >= 128:
                        src = ins[name].rearrange("(n p) c -> n p c", p=128)[i, :, j0:j0 + w]
                        dst = outs[name].rearrange("(n p) c -> n p c", p=128)[i, :, j0:j0 + w]
                    else:
                        src = ins[name][:, j0:j0 + w]
                        dst = outs[name][:, j0:j0 + w]
                    ld = P.op("sync", lambda e, s=src, t=stg[sl], rr=rows, ww=w:
                              e.dma_start(out=t[0:rr, 0:ww], in_=s),
                              waits=[free_ms[sl]], inc=P.group(), inc_amt=16)
                    cv = P.op("vector", lambda e, t=stg[sl], o=ob[sl], rr=rows, ww=w:
                              e.tensor_copy(o[0:rr, 0:ww], t[0:rr, 0:ww]),
                              waits=[ld, st_ms[sl]], inc="DV")
                    free_ms[sl] = cv
                    st = P.op("sync", lambda e, o=ob[sl], d=dst, rr=rows, ww=w:
                              e.dma_start(out=d, in_=o[0:rr, 0:ww]),
                              waits=[cv], inc=P.group(), inc_amt=16)
                    st_ms[sl] = st
        P.op("sync", lambda e: e.nop(), waits=[st_ms[0], st_ms[1], P.now("DV")])
        P.play(block)
    return nc


ARENA_COLS = 52992
DBG = {"mode": None, "nexp": EXP, "nseq": None, "mla_qbs": None, "mla_heads": None}


def carve(nc, ctx):
    z = Z()
    t = ctx.enter_context(nc.sbuf_tensor("arena", [128, ARENA_COLS], F32))
    z.arena = t
    off = [0]

    def cv(cols, dtype=F32):
        a = t[:, off[0]:off[0] + cols]
        off[0] += cols
        return a.bitcast(BF16) if dtype == BF16 else a

    z.CSTF = cv(CF_COLS)
    z.CSTB = cv(CB_COLS // 2, BF16)
    z.ROWS = cv(2048)
    z.ROWS2 = cv(1024)
    z.SCR = cv(512)
    z.SCB = cv(512, BF16)
    z.g1c = cv(8)
    z.g2c = cv(8)
    z.b1f = cv(128)
    z.b2f = cv(64)
    z.X = cv(16384)
    z.H = cv(16384)
    z.pool0 = off[0]
    z.xoff = z.pool0 - 32768
    z.hoff = z.pool0 - 16384

    z.seg64b = z.CSTB[:, CBCOL["seg64b"]:CBCOL["seg64b"] + 2]
    z.segqb = z.CSTB[:, CBCOL["segqb"]:CBCOL["segqb"] + 2]
    z.onesb = z.CSTB[:, CBCOL["onesb"]:CBCOL["onesb"] + 1]
    z.ones_row = z.CSTF[0:1, CFCOL["ones_row"]:CFCOL["ones_row"] + 128]

    def ones_row_at(base):
        return z.CSTF[base:base + 1, CFCOL["ones_row"]:CFCOL["ones_row"] + 128]
    z.ones_row_at = ones_row_at
    z.seg64 = z.CSTF[:, CFCOL["seg64"]:CFCOL["seg64"] + 2]
    z.ex2 = z.CSTF[0:2, CFCOL["ex2"]:CFCOL["ex2"] + 128]
    z.segq = z.CSTF[:, CFCOL["segq"]:CFCOL["segq"] + 2]
    z.exq = z.CSTF[0:2, CFCOL["exq"]:CFCOL["exq"] + 128]
    z.ident = z.CSTF[:, CFCOL["ident"]:CFCOL["ident"] + 128]

    def cst1(name, rows=128):
        return z.CSTF[0:rows, CFCOL[name]:CFCOL[name] + 1]
    z.cst1 = cst1

    z.pb = [ctx.enter_context(nc.psum_tensor(f"pb{i}", [128, 512], F32))[:, :]
            for i in range(8)]
    return z


def pool(z, use):
    reg_lo = z.xoff if use == "x" else z.hoff
    state = {"alt": 0, "off": z.pool0}

    def cv(cols, dtype=F32):
        if state["alt"] + cols <= 16384:
            a = z.arena[:, reg_lo + state["alt"]: reg_lo + state["alt"] + cols]
            state["alt"] += cols
        else:
            assert state["off"] + cols <= ARENA_COLS, (state, cols)
            a = z.arena[:, state["off"]:state["off"] + cols]
            state["off"] += cols
        return a.bitcast(BF16) if dtype == BF16 else a
    return cv


def carve_attn(zb, kind, use):
    import copy
    z = copy.copy(zb)
    cv = pool(z, use)
    if kind == "mla":
        z.kb = cv(8192, BF16)
        z.vb = cv(8320, BF16)
        z.xnq = cv(2048, BF16)
        z.wA = cv(2048, BF16)
        z.wBC = cv(2048, BF16)
        z.wD = cv(4096, BF16)
        r0 = cv(2560, BF16)
        z.cbuf = r0[:, 0:2048]
        z.ptb = r0[:, 0:1024]
        z.ao = r0[:, 1024:5120]
        z.cmask = cv(1024, BF16)
    elif kind == "a1":
        z.kb = cv(8192, BF16)
        z.vb = cv(8320, BF16)
        z.xnq = cv(2048, BF16)
        z.ptb = cv(512, BF16)
        z.ao = cv(2048, BF16)
        z.wD = cv(4096, BF16)
        z.wqT = cv(32, BF16)
        z.wkT = cv(32, BF16)
        z.wvT = cv(32, BF16)
        z.cmask = cv(1024, BF16)
    elif kind == "a0":
        z.kb16 = cv(8192, BF16)
        z.vb = cv(8320, BF16)
        z.xnq = cv(2048, BF16)
        z.ptb = cv(512, BF16)
        z.ao = cv(2048, BF16)
        z.wQ0 = cv(1024, BF16)
        z.wK0 = cv(1024, BF16)
        z.wV0 = cv(4096, BF16)
        z.wD = z.wV0
        z.GT = cv(64)
    return z


def carve_moe(zb, use):
    import copy
    z = copy.copy(zb)
    cv = pool(z, use)
    z.xn = cv(8192, BF16)
    z.wE1 = cv(8192, BF16)
    z.wE2 = cv(8192, BF16)
    z.hid = cv(4096, BF16)
    z.wG = cv(64, BF16)
    z.GLQ = cv(512)[0:8, :]
    z.CFM = cv(1024, BF16)[0:8, :]
    z.GT = cv(64)
    z.CBS = cv(512)
    z.MSCR = z.SCR
    z.sel8 = cv(512, BF16)[0:8, :]
    return z


def xsl(buf, fci, t0, n):
    return buf[:, fci * S + t0: fci * S + t0 + n]


def qsl(buf, fci):
    return buf[:, fci * QB:(fci + 1) * QB]


def emit_weight_load(P, dst_sb, dram, K, M, waits=(), queue="sync"):
    kc = max(1, K // 128)
    rows = min(K, 128)
    sem = P.group()
    ms = None
    src3 = dram.rearrange("(n p) c -> n p c", p=128) if K >= 128 else None
    for i in range(kc):
        src = src3[i] if K >= 128 else dram
        ms = P.op(queue, lambda e, s=src, d=dst_sb, i=i, rr=rows, M=M:
                  e.dma_start(out=d[0:rr, i * M:(i + 1) * M], in_=s),
                  waits=(waits if i == 0 else ()), inc=sem, inc_amt=16)
    return ms


def emit_ln_rows(P, z, src, *, rms=False, eps):
    for qb in range(NQB):
        t0 = qb * QB
        sx_ms = sxx_ms = None
        for fci in range(FC):
            cvt = P.op("vector", lambda e, fci=fci, t0=t0:
                       e.tensor_copy(z.SCB[:, 0:QB], xsl(src, fci, t0, QB)),
                       waits=[P.ms_scb_free, P.ms_scb_sq], inc="DV")
            sq = P.op("scalar", lambda e:
                      e.activation(z.SCB[:, QB:2 * QB], z.SCB[:, 0:QB], AF.Square),
                      waits=[cvt, P.ms_scb2_free], inc="AC")
            P.ms_scb_sq = sq
            if not rms:
                sx_ms = P.op("tensor", lambda e, fci=fci:
                             e.matmul(z.pb[6][0:1, :], z.onesb, z.SCB[:, 0:QB],
                                      start=(fci == 0), stop=(fci == FC - 1)),
                             waits=([cvt, P.bank_free[6]] if fci == 0 else [cvt]),
                             inc="PE")
            sxx_ms = P.op("tensor", lambda e, fci=fci:
                          e.matmul(z.pb[7][0:1, :], z.onesb, z.SCB[:, QB:2 * QB],
                                   start=(fci == 0), stop=(fci == FC - 1)),
                          waits=([sq, P.bank_free[7]] if fci == 0 else [sq]),
                          inc="PE")
            P.ms_scb_free = sxx_ms if rms else sx_ms
            P.ms_scb2_free = sxx_ms
        tA = z.ROWS2[0:1, 0:QB]
        tB = z.ROWS2[0:1, 512:512 + QB]
        tC = z.SCR[0:1, 0:QB]
        rr = lambda base, t0=t0: z.ROWS[base:base + 1, t0:t0 + QB]
        if not rms:
            m1 = P.op("vector", lambda e: e.tensor_scalar(
                tA, z.pb[6][0:1, :], 1.0 / D, None, OP.mult),
                waits=[sx_ms, P.ms_rowtemp], inc="DV")
            e1 = P.op("vector", lambda e: e.tensor_scalar(
                tB, z.pb[7][0:1, :], 1.0 / D, None, OP.mult),
                waits=[sxx_ms, P.ms_rowtemp], inc="DV")
            P.bank_free[6] = m1
            P.bank_free[7] = e1
            v1 = P.op("vector", lambda e: e.tensor_tensor(
                tC, tA, tA, OP.mult), waits=[m1, P.ms_scr_free], inc="DV")
            v2 = P.op("vector", lambda e: e.tensor_tensor(
                tB, tB, tC, OP.subtract), waits=[v1, e1], inc="DV")
            l1 = P.op("scalar", lambda e: e.activation(
                tB, tB, AF.Ln, bias=z.cst1(eps, rows=1)), waits=[v2], inc="AC")
            l2 = P.op("scalar", lambda e, rr=rr: e.activation(
                rr(0), tB, AF.Exp, scale=-0.5), waits=[l1], inc="AC")
            v3 = P.op("vector", lambda e, rr=rr: e.tensor_tensor(
                tC, tA, rr(0), OP.mult), waits=[l2], inc="DV")
            P.ms_scr_free = v3
            mv = P.op("tensor", lambda e: e.matmul(
                z.pb[6][:, :], z.ones_row_at(0), tC, start=True, stop=True),
                waits=[v3, P.bank_free[6]], inc="PE")
            mvc = P.op("vector", lambda e, rr=rr: e.tensor_copy(
                rr(32), z.pb[6][32:33, :]), waits=[mv], inc="DV")
            P.bank_free[6] = mvc
            P.ms_rowtemp = mvc
        else:
            e1 = P.op("vector", lambda e: e.tensor_scalar(
                tB, z.pb[7][0:1, :], 1.0 / D, None, OP.mult),
                waits=[sxx_ms, P.ms_rowtemp], inc="DV")
            P.bank_free[7] = e1
            l1 = P.op("scalar", lambda e: e.activation(
                tB, tB, AF.Ln, bias=z.cst1(eps, rows=1)), waits=[e1], inc="AC")
            l2 = P.op("scalar", lambda e, rr=rr: e.activation(
                rr(0), tB, AF.Exp, scale=-0.5), waits=[l1], inc="AC")
            P.ms_rowtemp = l2
    return P.now("DV"), P.now("AC")


def emit_norm_qb(P, z, src, dst_slicer, t0, *, kind, g_col=None, row_waits=()):
    rsrc = 0
    b6 = P.op("tensor", lambda e: e.matmul(
        z.pb[6][:, :], z.ones_row_at(rsrc), z.ROWS[rsrc:rsrc + 1, t0:t0 + QB],
        start=True, stop=True),
        waits=list(row_waits) + [P.bank_free[6]], inc="PE")
    if kind == "ln":
        b7 = P.op("tensor", lambda e: e.matmul(
            z.pb[7][:, :], z.ones_row_at(32), z.ROWS[32:33, t0:t0 + QB],
            start=True, stop=True),
            waits=list(row_waits) + [P.bank_free[7]], inc="PE")
    last = None
    for fci in range(FC):
        p1 = P.op("vector", lambda e, fci=fci: e.tensor_tensor(
            z.SCR, xsl(src, fci, t0, QB), z.pb[6][:, :], OP.mult),
            waits=[b6, P.ms_scr_free], inc="DV")
        if kind == "ln":
            p2 = P.op("vector", lambda e, fci=fci: e.tensor_tensor(
                dst_slicer(fci), z.SCR, z.pb[7][:, :], OP.subtract),
                waits=[p1, b7], inc="DV")
        else:
            p2 = P.op("vector", lambda e, fci=fci: e.tensor_scalar(
                dst_slicer(fci), z.SCR, g_col(fci), None, OP.mult),
                waits=[p1], inc="DV")
        P.ms_scr_free = p2
        last = p2
    P.bank_free[6] = last
    if kind == "ln":
        P.bank_free[7] = last
    return last


def emit_sqmax(P, z, src_slicer, nchunks, segb, dst_col, *, init, bank=6, ready=()):
    run = None
    for i in range(nchunks):
        sq = P.op("scalar", lambda e, i=i:
                  e.activation(z.SCB[:, 0:QB], src_slicer(i), AF.Square),
                  waits=[P.ms_scb_free] + list(ready), inc="AC")
        mmq = P.op("tensor", lambda e: e.matmul(
            z.pb[bank][0:2, :], segb, z.SCB[:, 0:QB], start=True, stop=True),
            waits=[sq, P.bank_free[bank]], inc="PE")
        P.ms_scb_free = mmq
        rmx = P.op("vector", lambda e: e.tensor_reduce(
            z.ROWS2[0:2, dst_col + 1:dst_col + 2], z.pb[bank][0:2, :], axis=AX.X, op=OP.max),
            waits=[mmq], inc="DV")
        P.bank_free[bank] = rmx
        if init and i == 0:
            run = P.op("vector", lambda e: e.tensor_copy(
                z.ROWS2[0:2, dst_col:dst_col + 1], z.ROWS2[0:2, dst_col + 1:dst_col + 2]),
                waits=[rmx], inc="DV")
        else:
            run = P.op("vector", lambda e: e.tensor_tensor(
                z.ROWS2[0:2, dst_col:dst_col + 1], z.ROWS2[0:2, dst_col:dst_col + 1],
                z.ROWS2[0:2, dst_col + 1:dst_col + 2], OP.max),
                waits=[rmx, run], inc="DV")
    return run


def emit_bias_from_sqmax(P, z, mk_ms, mq_ms, *, bank=6):
    tp1 = P.op("tensor", lambda e: e.transpose(
        z.pb[bank][0:1, 0:2], z.ROWS2[0:2, 8:9], z.ident[0:2, 0:2]),
        waits=[mk_ms, P.bank_free[bank]], inc="PE")
    tp2 = P.op("tensor", lambda e: e.transpose(
        z.pb[bank][0:1, 2:4], z.ROWS2[0:2, 12:13], z.ident[0:2, 0:2]),
        waits=[mq_ms], inc="PE")
    r1 = P.op("vector", lambda e: e.tensor_reduce(
        z.ROWS2[0:1, 20:21], z.pb[bank][0:1, 0:2], axis=AX.X, op=OP.max),
        waits=[tp1, tp2], inc="DV")
    r2 = P.op("vector", lambda e: e.tensor_reduce(
        z.ROWS2[0:1, 21:22], z.pb[bank][0:1, 2:4], axis=AX.X, op=OP.max),
        waits=[tp1, tp2], inc="DV")
    r3 = P.op("vector", lambda e: e.tensor_tensor(
        z.ROWS2[0:1, 20:21], z.ROWS2[0:1, 20:21], z.ROWS2[0:1, 21:22], OP.mult),
        waits=[r1, r2], inc="DV")
    r4 = P.op("scalar", lambda e: e.activation(
        z.ROWS2[0:1, 20:21], z.ROWS2[0:1, 20:21], AF.Ln, bias=z.cst1("eps_l2", rows=1)),
        waits=[r3], inc="AC")
    r5 = P.op("scalar", lambda e: e.activation(
        z.ROWS2[0:1, 20:21], z.ROWS2[0:1, 20:21], AF.Exp, scale=0.5),
        waits=[r4], inc="AC")
    r6 = P.op("scalar", lambda e: e.mul(
        z.ROWS2[0:1, 20:21], z.ROWS2[0:1, 20:21], -0.125), waits=[r5], inc="AC")
    bb = P.op("tensor", lambda e: e.matmul(
        z.pb[bank][:, 0:1], z.ones_row, z.ROWS2[0:1, 20:21], start=True, stop=True),
        waits=[r6], inc="PE")
    bias_ms = P.op("vector", lambda e: e.tensor_copy(z.ROWS2[:, 16:17], z.pb[bank][:, 0:1]),
                   waits=[bb], inc="DV")
    P.bank_free[bank] = bias_ms
    return bias_ms, z.ROWS2[:, 16:17]


def emit_attn_qb(P, z, *, qb, q_src, kb, vb, bias, bias_ms, q_ready, kv_ready,
                 out_target, wout, wout_ready, scale=None, wout_wM=D):
    n_kc = 4 * qb + 4
    ao_last = None
    nheads = DBG["mla_heads"] or HEADS
    for h in range(nheads):
        ch, sl = h // 2, (h % 2) * 64
        for kc in range(n_kc):
            sbank = kc % 2
            smm = P.op("tensor", lambda e, kc=kc, ch=ch, sl=sl, sbank=sbank:
                       e.matmul(z.pb[sbank][:, :],
                                xsl(kb, ch, kc * 128, 128)[sl:sl + 64, :],
                                q_src(ch)[sl:sl + 64, :],
                                start=True, stop=True),
                       waits=[q_ready, kv_ready, P.bank_free[sbank]], inc="PE")
            pt = z.ptb[:, (kc % 2) * QB:(kc % 2) * QB + QB]
            ex = P.op("scalar", lambda e, pt=pt, sbank=sbank:
                      e.activation(pt, z.pb[sbank][:, :], AF.Exp, bias=bias, scale=0.125),
                      waits=[smm, bias_ms, P.pt_free[kc % 2]], inc="AC")
            P.bank_free[sbank] = ex
            lastp = ex
            if kc >= 4 * qb:
                ri = kc - 4 * qb
                lastp = P.op("vector", lambda e, pt=pt, ri=ri:
                             e.tensor_tensor(pt, pt, z.cmask[:, ri * QB:(ri + 1) * QB], OP.mult),
                             waits=[ex, P.cmask_ms], inc="DV")
            av_waits = [lastp] + ([P.bank_free[2]] if kc == 0 else [])
            avm = P.op("tensor", lambda e, kc=kc, h=h, pt=pt, n_kc=n_kc:
                       e.matmul(z.pb[2][0:65, :],
                                vb[:, (kc * HEADS + h) * 65:(kc * HEADS + h) * 65 + 65],
                                pt, start=(kc == 0), stop=(kc == n_kc - 1)),
                       waits=av_waits, inc="PE")
            P.pt_free[kc % 2] = avm
        rc = P.op("vector", lambda e: e.reciprocal(z.ROWS2[64:65, QB:2 * QB],
                                                   z.pb[2][64:65, :]),
                  waits=[avm], inc="DV")
        bc = P.op("tensor", lambda e: e.matmul(
            z.pb[3][:, :], z.ones_row_at(64), z.ROWS2[64:65, QB:2 * QB],
            start=True, stop=True),
            waits=[rc, P.bank_free[3]], inc="PE")
        bce = P.op("vector", lambda e: e.tensor_copy(z.SCR[0:64, 0:QB], z.pb[3][0:64, :]),
                   waits=[bc, P.ms_scr_free], inc="DV")
        P.bank_free[3] = bce
        ao = P.op("vector", lambda e, ch=ch, sl=sl:
                  e.tensor_tensor(z.ao[sl:sl + 64, ch * QB:(ch + 1) * QB],
                                  z.pb[2][0:64, :], z.SCR[0:64, 0:QB], OP.mult),
                  waits=[bce], inc="DV")
        P.ms_scr_free = ao
        P.bank_free[2] = ao
        ao_last = ao
    last = None
    if nheads < HEADS:
        return None
    for fci in range(FC):
        bank = 4 + fci % 2
        waits = [wout_ready, ao_last, P.bank_free[bank]]
        mm = None
        for c in range(FC):
            mm = P.op("tensor", lambda e, c=c, fci=fci, bank=bank:
                      e.matmul(z.pb[bank][:, :],
                               wout[:, c * wout_wM + fci * 128: c * wout_wM + fci * 128 + 128],
                               z.ao[:, c * QB:(c + 1) * QB],
                               start=(c == 0), stop=(c == FC - 1)),
                      waits=(waits if c == 0 else ()), inc=("PE" if c == FC - 1 else None))
        if scale is None:
            ev = P.op("vector", lambda e, fci=fci, qb=qb, bank=bank:
                      e.tensor_tensor(xsl(out_target, fci, qb * QB, QB),
                                      z.pb[bank][:, :], xsl(out_target, fci, qb * QB, QB),
                                      OP.add),
                      waits=[mm], inc="DV")
        else:
            ev = P.op("vector", lambda e, fci=fci, qb=qb, bank=bank:
                      e.scalar_tensor_tensor(xsl(out_target, fci, qb * QB, QB),
                                             z.pb[bank][:, :], scale,
                                             xsl(out_target, fci, qb * QB, QB),
                                             OP.mult, OP.add),
                      waits=[mm], inc="DV")
        P.bank_free[bank] = ev
        last = ev
    return last


def emit_mla(P, z, w):
    wdkv_ms = emit_weight_load(P, z.wA, w["wdkv"], D, LAT, waits=P.pool_free)
    cm_ms = P.op("sync", lambda e: e.dma_start(out=z.cmask, in_=P.cmask_dram),
                 waits=P.pool_free, inc=P.group(), inc_amt=16)
    P.cmask_ms = cm_ms
    ones_ms = P.op("gpsimd", lambda e: e.memset(
        z.vb.rearrange("p (k e) -> p k e", e=65)[:, :, 64:65], 1.0),
        waits=P.pool_free_g, inc="PL")
    mk_ms = None
    for qb in range(DBG["mla_qbs"] or NQB):
        t0 = qb * QB
        xn_ms = emit_norm_qb(P, z, z.H, lambda fci: qsl(z.xnq, fci), t0,
                             kind="rms", g_col=lambda fci: z.g1c[:, fci:fci + 1])
        c_ms = None
        for mc in range(4):
            bank = mc % 2
            mm = None
            for kc in range(FC):
                mm = P.op("tensor", lambda e, kc=kc, mc=mc, bank=bank:
                          e.matmul(z.pb[bank][:, :],
                                   z.wA[:, kc * LAT + mc * 128: kc * LAT + mc * 128 + 128],
                                   qsl(z.xnq, kc), start=(kc == 0), stop=(kc == FC - 1)),
                          waits=([wdkv_ms, xn_ms, P.bank_free[bank]] if kc == 0 else ()),
                          inc=("PE" if kc == FC - 1 else None))
            c_ms = P.op("vector", lambda e, mc=mc, bank=bank:
                        e.tensor_copy(qsl(z.cbuf, mc), z.pb[bank][:, :]),
                        waits=[mm], inc="DV")
            P.bank_free[bank] = c_ms
        wuk_ms = emit_weight_load(P, z.wBC, w["wuk"], LAT, D, waits=[P.now("PE")])
        k_ms = None
        for mc in range(FC):
            bank = 2 + mc % 2
            mm = None
            for kc in range(4):
                mm = P.op("tensor", lambda e, kc=kc, mc=mc, bank=bank:
                          e.matmul(z.pb[bank][:, :],
                                   z.wBC[:, kc * D + mc * 128: kc * D + mc * 128 + 128],
                                   qsl(z.cbuf, kc), start=(kc == 0), stop=(kc == 3)),
                          waits=([wuk_ms, c_ms, P.bank_free[bank]] if kc == 0 else ()),
                          inc=("PE" if kc == 3 else None))
            k_ms = P.op("vector", lambda e, mc=mc, bank=bank, t0=t0:
                        e.tensor_copy(xsl(z.kb, mc, t0, QB), z.pb[bank][:, :]),
                        waits=[mm], inc="DV")
            P.bank_free[bank] = k_ms
        wuv_ms = emit_weight_load(P, z.wBC, w["wuv"], LAT, D, waits=[P.now("PE")])
        for tt in range(4):
            kc_abs = qb * 4 + tt
            for vh in range(2):
                bank = 4 + (tt * 2 + vh) % 2
                mm = None
                for lc in range(4):
                    mm = P.op("tensor", lambda e, lc=lc, tt=tt, vh=vh, bank=bank:
                              e.matmul(z.pb[bank][:, :],
                                       qsl(z.cbuf, lc)[:, tt * 128:tt * 128 + 128],
                                       z.wBC[:, lc * D + vh * 512: lc * D + vh * 512 + 512],
                                       start=(lc == 0), stop=(lc == 3)),
                              waits=([c_ms, wuv_ms, ones_ms, P.bank_free[bank]]
                                     if lc == 0 else ()),
                              inc=("PE" if lc == 3 else None))
                dst = z.vb[:, (kc_abs * HEADS + vh * 8) * 65:(kc_abs * HEADS + vh * 8 + 8) * 65]
                dst = dst.rearrange("p (h e) -> p h e", e=65)[:, :, 0:64]
                v_ms = P.op("vector", lambda e, dst=dst, bank=bank:
                            e.tensor_copy(dst,
                                          z.pb[bank][:, :].rearrange("p (h e) -> p h e", e=64)),
                            waits=[mm], inc="DV")
                P.bank_free[bank] = v_ms
        mk_ms = emit_sqmax(P, z, lambda i, t0=t0: xsl(z.kb, i, t0, QB), FC,
                           z.seg64b, 8, init=(qb == 0), ready=[P.now("DV")])
        wq_ms = emit_weight_load(P, z.wD, w["wq"], D, D, waits=[P.now("PE")])
        mms = []
        for mc in range(FC):
            mm = None
            for kc in range(FC):
                mm = P.op("tensor", lambda e, kc=kc, mc=mc:
                          e.matmul(z.pb[mc][:, :],
                                   z.wD[:, kc * D + mc * 128: kc * D + mc * 128 + 128],
                                   qsl(z.xnq, kc), start=(kc == 0), stop=(kc == FC - 1)),
                          waits=([wq_ms, xn_ms, P.bank_free[mc]] if kc == 0 else ()),
                          inc=("PE" if kc == FC - 1 else None))
            mms.append(mm)
        qev = None
        for mc in range(FC):
            qev = P.op("vector", lambda e, mc=mc:
                       e.tensor_copy(qsl(z.xnq, mc), z.pb[mc][:, :]),
                       waits=[mms[-1]], inc="DV")
            P.bank_free[mc] = qev
        mq_ms = emit_sqmax(P, z, lambda i: qsl(z.xnq, i), FC, z.seg64b, 12, init=True,
                           ready=[P.now("DV")])
        bias_ms, bias = emit_bias_from_sqmax(P, z, mk_ms, mq_ms)
        wo_ms = emit_weight_load(P, z.wD, w["wo"], D, D, waits=[P.now("PE")])
        emit_attn_qb(P, z, qb=qb, q_src=lambda ch: qsl(z.xnq, ch),
                     kb=z.kb, vb=z.vb, bias=bias, bias_ms=bias_ms,
                     q_ready=P.now("DV"), kv_ready=P.now("DV"),
                     out_target=z.H, wout=z.wD, wout_ready=wo_ms, scale=None)


def emit_moe(P, z, w, *, variant, target):
    l1 = variant == "l1"
    gate_ms = emit_weight_load(P, z.wG, w["gate"], D, 8, waits=P.pool_free)
    sel_ms = P.op("sync", lambda e: e.dma_start(out=z.sel8, in_=P.sel8_dram),
                  waits=P.pool_free, inc=P.group(), inc_amt=16)
    for qb in range(NQB):
        if not l1:
            emit_norm_qb(P, z, z.H,
                         lambda fci, qb=qb: z.xn[:, fci * S + qb * QB: fci * S + qb * QB + QB],
                         qb * QB, kind="rms", g_col=lambda fci: z.g2c[:, fci:fci + 1])
        else:
            emit_norm_qb(P, z, z.X,
                         lambda fci, qb=qb: z.xn[:, fci * S + qb * QB: fci * S + qb * QB + QB],
                         qb * QB, kind="ln")
    xn_all = P.now("DV")
    cfm_ms = None
    for qb in range(NQB):
        mm = None
        for kc in range(FC):
            mm = P.op("tensor", lambda e, kc=kc, qb=qb:
                      e.matmul(z.pb[6][0:8, :],
                               z.wG[:, kc * 8: kc * 8 + 8],
                               z.xn[:, kc * S + qb * QB: kc * S + qb * QB + QB],
                               start=(kc == 0), stop=(kc == FC - 1)),
                      waits=([gate_ms, xn_all, P.bank_free[6]] if kc == 0 else ()),
                      inc=("PE" if kc == FC - 1 else None))
        gl_ms = P.op("vector", lambda e, qb=qb:
                     e.tensor_copy(z.GLQ, z.pb[6][0:8, :]),
                     waits=[mm, P.ms_glq_free], inc="DV")
        P.bank_free[6] = gl_ms
        if DBG["mode"] == "gatemm":
            continue
        for ttl in range(4):
            tt = qb * 4 + ttl
            tp = P.op("tensor", lambda e, ttl=ttl: e.transpose(
                z.pb[7][:, 0:8], z.GLQ[:, ttl * 128:(ttl + 1) * 128], z.ident[0:8, 0:8]),
                waits=[gl_ms, P.bank_free[7]], inc="PE")
            P.ms_glq_free = tp
            gt = P.op("vector", lambda e: e.tensor_copy(z.GT[:, 0:8], z.pb[7][:, 0:8]),
                      waits=[tp, P.ms_gt_free, P.ms_gt_ac], inc="DV")
            P.bank_free[7] = gt
            P.ms_gt_free = gt
            if DBG["mode"] == "gatetp":
                continue
            col = lambda c: z.GT[:, c:c + 1]
            blk = lambda c0: z.GT[:, c0:c0 + 8]
            if not l1:
                m1r = P.op("vector", lambda e, col=col, blk=blk: e.tensor_reduce(
                    col(8), blk(0), axis=AX.X, op=OP.max), waits=[gt], inc="DV")
                P.op("vector", lambda e, col=col, blk=blk: e.tensor_scalar(
                    blk(16), blk(0), col(8), 0.0, OP.subtract, OP.is_ge),
                    waits=[m1r], inc="DV")
                P.op("vector", lambda e, col=col, blk=blk: e.scalar_tensor_tensor(
                    blk(24), blk(16), -1e30, blk(0), OP.mult, OP.add), inc="DV")
                m2r = P.op("vector", lambda e, col=col, blk=blk: e.tensor_reduce(
                    col(9), blk(24), axis=AX.X, op=OP.max), inc="DV")
                dd = P.op("vector", lambda e, col=col: e.tensor_tensor(
                    col(10), col(9), col(8), OP.subtract), inc="DV")
                ee = P.op("scalar", lambda e, col=col: e.activation(col(11), col(10), AF.Exp),
                          waits=[dd], inc="AC")
                P.ms_gt_ac = ee
                dn = P.op("scalar", lambda e, col=col: e.add(
                    col(12), col(11), 1.0), waits=[ee], inc="AC")
                rdr = P.op("vector", lambda e, col=col: e.reciprocal(col(13), col(12)),
                           waits=[dn], inc="DV")
                c2r = P.op("vector", lambda e, col=col: e.tensor_tensor(
                    col(14), col(11), col(13), OP.mult), inc="DV")
                P.op("vector", lambda e, col=col, blk=blk: e.tensor_scalar(
                    blk(32), blk(24), col(9), 0.0, OP.subtract, OP.is_ge),
                    waits=[m2r], inc="DV")
                P.op("vector", lambda e, col=col, blk=blk: e.tensor_scalar(
                    blk(40), blk(16), col(13), None, OP.mult), waits=[rdr], inc="DV")
                P.op("vector", lambda e, col=col, blk=blk: e.tensor_scalar(
                    blk(48), blk(32), col(14), None, OP.mult), waits=[c2r], inc="DV")
                ct = P.op("vector", lambda e, blk=blk: e.tensor_tensor(
                    blk(40), blk(40), blk(48), OP.add), inc="DV")
            else:
                nm = P.op("vector", lambda e, col=col, blk=blk: e.tensor_reduce(
                    col(9), blk(0), axis=AX.X, op=OP.max, negate=True),
                    waits=[gt], inc="DV")
                eL = P.op("scalar", lambda e, col=col, blk=blk: e.activation(
                    blk(16), blk(0), AF.Exp, bias=col(9)), waits=[nm], inc="AC")
                sr = P.op("vector", lambda e, col=col, blk=blk: e.tensor_reduce(
                    col(10), blk(16), axis=AX.X, op=OP.add), waits=[eL], inc="DV")
                rsr = P.op("vector", lambda e, col=col: e.reciprocal(col(11), col(10)),
                           waits=[sr], inc="DV")
                P.op("vector", lambda e, col=col, blk=blk: e.tensor_scalar(
                    blk(24), blk(16), col(11), None, OP.mult), waits=[rsr], inc="DV")
                m12 = P.op("vector", lambda e, col=col, blk=blk: e.tensor_reduce(
                    col(12), blk(24), axis=AX.X, op=OP.max), inc="DV")
                P.op("vector", lambda e, col=col, blk=blk: e.tensor_scalar(
                    blk(32), blk(24), col(12), 0.0, OP.subtract, OP.is_ge),
                    waits=[m12], inc="DV")
                P.op("vector", lambda e, col=col, blk=blk: e.scalar_tensor_tensor(
                    blk(40), blk(32), -1e30, blk(24), OP.mult, OP.add), inc="DV")
                m13 = P.op("vector", lambda e, col=col, blk=blk: e.tensor_reduce(
                    col(13), blk(40), axis=AX.X, op=OP.max), inc="DV")
                P.op("vector", lambda e, col=col, blk=blk: e.tensor_scalar(
                    blk(48), blk(40), col(13), 0.0, OP.subtract, OP.is_ge),
                    waits=[m13], inc="DV")
                P.op("vector", lambda e, col=col, blk=blk: e.tensor_scalar(
                    blk(40), blk(32), col(12), None, OP.mult), waits=[m12], inc="DV")
                P.op("vector", lambda e, col=col, blk=blk: e.tensor_scalar(
                    blk(48), blk(48), col(13), None, OP.mult), waits=[m13], inc="DV")
                ct = P.op("vector", lambda e, blk=blk: e.tensor_tensor(
                    blk(40), blk(40), blk(48), OP.add), inc="DV")
            P.ms_gt_ac = P.now("AC")
            tpc = P.op("tensor", lambda e: e.transpose(
                z.pb[7][0:8, 0:128], z.GT[:, 40:48], z.ident),
                waits=[ct, P.bank_free[7]], inc="PE")
            cfm_ms = P.op("vector", lambda e, tt=tt: e.tensor_copy(
                z.CFM[0:8, tt * 128:(tt + 1) * 128], z.pb[7][0:8, 0:128]),
                waits=[tpc], inc="DV")
            P.bank_free[7] = cfm_ms
            P.ms_gt_free = cfm_ms
    if DBG["mode"] == "gatec":
        P.op("vector", lambda e: e.tensor_copy(
            xsl(z.X, 0, 0, 2048)[0:8, :].bitcast(BF16)[:, 0:2048], z.CFM[0:8, :]),
            waits=[cfm_ms], inc="DV")
        P.op("vector", lambda e: e.tensor_copy(xsl(z.X, 1, 0, 64), z.GT),
            waits=[cfm_ms], inc="DV")
        return P.now("DV")
    if DBG["mode"] in ("gate", "gatemm", "gatetp"):
        return P.now("DV")
    w1_3 = w["w1"]
    w2_3 = w["w2"]
    w1_ms = emit_weight_load(P, z.wE1, w1_3[0:D, :], D, EDIM, waits=P.pool_free)
    ye_prev = None
    for e_i in range(DBG["nexp"]):
        w2_ms = emit_weight_load(P, z.wE2, w2_3[e_i * EDIM:(e_i + 1) * EDIM, :], EDIM, D,
                                 waits=[ye_prev] + list(P.pool_free))
        for tq in range(4):
            for hcq in range(16):
                bank = (tq * 16 + hcq) % 4
                mm = None
                for kc in range(FC):
                    mm = P.op("tensor", lambda e, kc=kc, hcq=hcq, tq=tq, bank=bank:
                              e.matmul(z.pb[bank][:, :],
                                       z.wE1[:, kc * EDIM + hcq * 128: kc * EDIM + hcq * 128 + 128],
                                       z.xn[:, kc * S + tq * QB: kc * S + tq * QB + QB],
                                       start=(kc == 0), stop=(kc == FC - 1)),
                              waits=([w1_ms, xn_all, P.bank_free[bank]] if kc == 0 else ()),
                              inc=("PE" if kc == FC - 1 else None))
                if l1:
                    ev = P.op("scalar", lambda e, hcq=hcq, bank=bank, e_i=e_i:
                              e.activation(z.hid[:, hcq * QB:hcq * QB + QB],
                                           z.pb[bank][:, :], AF.Relu,
                                           bias=z.b1f[:, e_i * 16 + hcq: e_i * 16 + hcq + 1]),
                              waits=[mm, P.hid_guard], inc="AC")
                else:
                    ev = P.op("scalar", lambda e, hcq=hcq, bank=bank:
                              e.activation(z.hid[:, hcq * QB:hcq * QB + QB],
                                           z.pb[bank][:, :], AF.Silu),
                              waits=[mm, P.hid_guard], inc="AC")
                P.bank_free[bank] = ev
            cb_mm = P.op("tensor", lambda e, e_i=e_i, tq=tq: e.matmul(
                z.pb[6][:, :], z.sel8[:, e_i * 128:(e_i + 1) * 128],
                z.CFM[0:8, tq * QB:(tq + 1) * QB], start=True, stop=True),
                waits=[cfm_ms, sel_ms, P.bank_free[6]], inc="PE")
            cb_ms = P.op("vector", lambda e: e.tensor_copy(z.CBS, z.pb[6][:, :]),
                         waits=[cb_mm, P.ms_cbs_free], inc="DV")
            P.bank_free[6] = cb_ms
            hid_q = P.now("AC")
            p2 = None
            for fci in range(FC):
                bank = 4 + fci % 2
                mm = None
                for hcq in range(16):
                    mm = P.op("tensor", lambda e, hcq=hcq, fci=fci, bank=bank:
                              e.matmul(z.pb[bank][:, :],
                                       z.wE2[:, hcq * D + fci * 128: hcq * D + fci * 128 + 128],
                                       z.hid[:, hcq * QB:hcq * QB + QB],
                                       start=(hcq == 0), stop=(hcq == 15)),
                              waits=([w2_ms, hid_q, P.bank_free[bank]] if hcq == 0 else ()),
                              inc=("PE" if hcq == 15 else None))
                if l1:
                    p1 = P.op("vector", lambda e, fci=fci, e_i=e_i, bank=bank:
                              e.scalar_tensor_tensor(z.MSCR, z.pb[bank][:, :],
                                                     z.b2f[:, e_i * 8 + fci: e_i * 8 + fci + 1],
                                                     z.CBS, OP.add, OP.mult),
                              waits=[mm, cb_ms, P.ms_scr_free], inc="DV")
                    p2 = P.op("vector", lambda e, fci=fci, tq=tq:
                              e.scalar_tensor_tensor(xsl(target, fci, tq * QB, QB),
                                                     z.MSCR, 0.5,
                                                     xsl(target, fci, tq * QB, QB),
                                                     OP.mult, OP.add),
                              waits=[p1], inc="DV")
                else:
                    p1 = P.op("vector", lambda e, fci=fci, bank=bank:
                              e.tensor_tensor(z.MSCR, z.pb[bank][:, :], z.CBS, OP.mult),
                              waits=[mm, cb_ms, P.ms_scr_free], inc="DV")
                    p2 = P.op("vector", lambda e, fci=fci, tq=tq:
                              e.tensor_tensor(xsl(target, fci, tq * QB, QB),
                                              z.MSCR, xsl(target, fci, tq * QB, QB), OP.add),
                              waits=[p1], inc="DV")
                P.ms_scr_free = p2
                P.bank_free[bank] = p2
            P.ms_cbs_free = p2
            P.hid_guard = P.now("PE")
        ye_prev = P.now("PE")
        if e_i + 1 < DBG["nexp"]:
            w1_ms = emit_weight_load(P, z.wE1, w1_3[(e_i + 1) * D:(e_i + 2) * D, :], D, EDIM,
                                     waits=[P.now("PE")])
    return P.now("DV")


def emit_attn1(P, z, w):
    def load2(dst, dram):
        g = P.group()
        P.op("sync", lambda e: e.dma_start(out=dst[0:64, 0:64], in_=dram),
             waits=P.pool_free, inc=g, inc_amt=16)
        return P.op("sync", lambda e: e.dma_start(out=dst[64:128, 0:64], in_=dram),
                    waits=P.pool_free, inc=g, inc_amt=16)
    wq_ms = load2(z.wqT, w["wqT"])
    cm_ms = P.op("sync", lambda e: e.dma_start(out=z.cmask, in_=P.cmask_dram),
                 waits=P.pool_free, inc=P.group(), inc_amt=16)
    P.cmask_ms = cm_ms
    wk_ms = load2(z.wkT, w["wkT"])
    wv_ms = load2(z.wvT, w["wvT"])
    emit_ln_rows(P, z, z.X, rms=False, eps="eps_ln")
    wo_ms = emit_weight_load(P, z.wD, w["woT"], D, D, waits=P.pool_free)
    ones_ms = P.op("gpsimd", lambda e: e.memset(
        z.vb.rearrange("p (k e) -> p k e", e=65)[:, :, 64:65], 1.0),
        waits=P.pool_free_g, inc="PL")
    mk_ms = None
    for qb in range(NQB):
        t0 = qb * QB
        yn_ms = emit_norm_qb(P, z, z.X, lambda fci: qsl(z.xnq, fci), t0, kind="ln")
        k_ms = None
        for h in range(HEADS):
            ch, sl = h // 2, (h % 2) * 64
            bank = h % 2
            mm = P.op("tensor", lambda e, ch=ch, sl=sl, bank=bank:
                      e.matmul(z.pb[bank][sl:sl + 64, :], z.wkT[sl:sl + 64, 0:64],
                               qsl(z.xnq, ch)[sl:sl + 64, :], start=True, stop=True),
                      waits=[wk_ms, yn_ms, P.bank_free[bank]], inc="PE")
            k_ms = P.op("vector", lambda e, ch=ch, sl=sl, bank=bank, t0=t0:
                        e.tensor_copy(xsl(z.kb, ch, t0, QB)[sl:sl + 64, :],
                                      z.pb[bank][sl:sl + 64, :]),
                        waits=[mm], inc="DV")
            P.bank_free[bank] = k_ms
        for tt in range(4):
            kc_abs = qb * 4 + tt
            for h in range(HEADS):
                ch, sl = h // 2, (h % 2) * 64
                bank = 2 + h % 2
                mm = P.op("tensor", lambda e, ch=ch, sl=sl, tt=tt, bank=bank:
                          e.matmul(z.pb[bank][:, 0:64],
                                   qsl(z.xnq, ch)[sl:sl + 64, tt * 128:tt * 128 + 128],
                                   z.wvT[sl:sl + 64, 0:64], start=True, stop=True),
                          waits=[wv_ms, yn_ms, ones_ms, P.bank_free[bank]], inc="PE")
                dst = z.vb[:, (kc_abs * HEADS + h) * 65:(kc_abs * HEADS + h) * 65 + 64]
                v_ms = P.op("vector", lambda e, dst=dst, bank=bank:
                            e.tensor_copy(dst, z.pb[bank][:, 0:64]),
                            waits=[mm], inc="DV")
                P.bank_free[bank] = v_ms
        mk_ms = emit_sqmax(P, z, lambda i, t0=t0: xsl(z.kb, i, t0, QB), FC,
                           z.seg64b, 8, init=(qb == 0), ready=[P.now("DV")])
        mms = []
        for h in range(HEADS):
            ch, sl = h // 2, (h % 2) * 64
            mm = P.op("tensor", lambda e, ch=ch, sl=sl:
                      e.matmul(z.pb[ch][sl:sl + 64, :], z.wqT[sl:sl + 64, 0:64],
                               qsl(z.xnq, ch)[sl:sl + 64, :], start=True, stop=True),
                      waits=[wq_ms, yn_ms, P.bank_free[ch]], inc="PE")
            mms.append(mm)
        qev = None
        for h in range(HEADS):
            ch, sl = h // 2, (h % 2) * 64
            qev = P.op("vector", lambda e, ch=ch, sl=sl:
                       e.tensor_copy(qsl(z.xnq, ch)[sl:sl + 64, :],
                                     z.pb[ch][sl:sl + 64, :]),
                       waits=[mms[-1]], inc="DV")
            P.bank_free[ch] = qev
        mq_ms = emit_sqmax(P, z, lambda i: qsl(z.xnq, i), FC, z.seg64b, 12, init=True,
                           ready=[P.now("DV")])
        bias_ms, bias = emit_bias_from_sqmax(P, z, mk_ms, mq_ms)
        emit_attn_qb(P, z, qb=qb, q_src=lambda ch: qsl(z.xnq, ch),
                     kb=z.kb, vb=z.vb, bias=bias, bias_ms=bias_ms,
                     q_ready=P.now("DV"), kv_ready=P.now("DV"),
                     out_target=z.X, wout=z.wD, wout_ready=wo_ms, scale=None)


def emit_attn0(P, z, w, *, first_seq):
    wq_ms = emit_weight_load(P, z.wQ0, w["wq"], D, 256, waits=P.pool_free)
    wk_ms = emit_weight_load(P, z.wK0, w["wk"], D, 256, waits=P.pool_free)
    wv_ms = emit_weight_load(P, z.wV0, w["wv"], D, D, waits=P.pool_free)
    emit_ln_rows(P, z, z.X, rms=False, eps="eps_ln")
    ones_ms = P.op("gpsimd", lambda e: e.memset(
        z.vb.rearrange("p (k e) -> p k e", e=65)[:, :, 64:65], 1.0),
        waits=P.pool_free_g, inc="PL")
    zm = P.op("gpsimd", lambda e: e.memset(z.kb16.bitcast(F32), 0.0),
              waits=P.pool_free_g, inc="PL")

    def l2norm_16(buf_slicer, ready):
        sq = P.op("scalar", lambda e: e.activation(z.SCB[:, 0:QB], buf_slicer(), AF.Square),
                  waits=[ready, P.ms_scb_free], inc="AC")
        mm = P.op("tensor", lambda e: e.matmul(
            z.pb[6][0:2, :], z.segqb, z.SCB[:, 0:QB], start=True, stop=True),
            waits=[sq, P.bank_free[6]], inc="PE")
        P.ms_scb_free = mm
        l1 = P.op("scalar", lambda e: e.activation(
            z.ROWS2[0:2, 512:512 + QB], z.pb[6][0:2, :], AF.Ln, bias=z.cst1("zero", rows=2)),
            waits=[mm, P.ms_rowtemp], inc="AC")
        P.bank_free[6] = l1
        l2 = P.op("scalar", lambda e: e.activation(
            z.ROWS2[0:2, 512:512 + QB], z.ROWS2[0:2, 512:512 + QB], AF.Exp, scale=0.5),
            waits=[l1], inc="AC")
        mx = P.op("vector", lambda e: e.tensor_scalar(
            z.ROWS2[0:2, 512:512 + QB], z.ROWS2[0:2, 512:512 + QB], 1e-12, None, OP.max),
            waits=[l2], inc="DV")
        rc = P.op("vector", lambda e: e.reciprocal(
            z.ROWS2[0:2, 512:512 + QB], z.ROWS2[0:2, 512:512 + QB]),
            waits=[mx], inc="DV")
        P.ms_rowtemp = rc
        ex = P.op("tensor", lambda e: e.matmul(
            z.pb[6][:, :], z.exq, z.ROWS2[0:2, 512:512 + QB], start=True, stop=True),
            waits=[rc, P.bank_free[6]], inc="PE")
        tt = P.op("vector", lambda e: e.tensor_tensor(
            buf_slicer(), buf_slicer(), z.pb[6][:, :], OP.mult),
            waits=[ex], inc="DV")
        P.bank_free[6] = tt
        return tt

    for qb in range(NQB):
        t0 = qb * QB
        yn_ms = emit_norm_qb(P, z, z.X, lambda fci: qsl(z.xnq, fci), t0, kind="ln")
        for h in range(HEADS):
            g, sl = h // 2, (h % 2) * 64
            bank = h % 2
            mm = None
            for kc in range(FC):
                mm = P.op("tensor", lambda e, h=h, kc=kc, sl=sl, bank=bank:
                          e.matmul(z.pb[bank][sl:sl + 16, :],
                                   z.wK0[:, kc * 256 + h * 16: kc * 256 + h * 16 + 16],
                                   qsl(z.xnq, kc), start=(kc == 0), stop=(kc == FC - 1)),
                          waits=([wk_ms, yn_ms, zm, P.bank_free[bank]] if kc == 0 else ()),
                          inc=("PE" if kc == FC - 1 else None))
            kev = P.op("vector", lambda e, g=g, sl=sl, bank=bank, t0=t0:
                       e.tensor_copy(z.kb16[sl:sl + 16, g * S + t0: g * S + t0 + QB],
                                     z.pb[bank][sl:sl + 16, :]),
                       waits=[mm], inc="DV")
            P.bank_free[bank] = kev
        for g in range(FC):
            l2norm_16(lambda g=g, t0=t0: z.kb16[:, g * S + t0: g * S + t0 + QB],
                      P.now("DV"))
        for tti in range(4):
            kc_abs = qb * 4 + tti
            for vh in range(2):
                bank = 2 + vh
                mm = None
                for kc in range(FC):
                    mm = P.op("tensor", lambda e, kc=kc, tti=tti, vh=vh, bank=bank:
                              e.matmul(z.pb[bank][:, :],
                                       qsl(z.xnq, kc)[:, tti * 128:tti * 128 + 128],
                                       z.wV0[:, kc * D + vh * 512: kc * D + vh * 512 + 512],
                                       start=(kc == 0), stop=(kc == FC - 1)),
                              waits=([wv_ms, yn_ms, ones_ms, P.bank_free[bank]]
                                     if kc == 0 else ()),
                              inc=("PE" if kc == FC - 1 else None))
                sq = P.op("scalar", lambda e, bank=bank:
                          e.activation(z.SCB[:, 0:QB], z.pb[bank][:, :], AF.Square),
                          waits=[mm, P.ms_scb_free], inc="AC")
                red = None
                for hh in range(8):
                    red = P.op("vector", lambda e, hh=hh: e.tensor_reduce(
                        z.GT[:, hh:hh + 1], z.SCB[:, hh * 64:hh * 64 + 64],
                        axis=AX.X, op=OP.add),
                        waits=[sq, P.ms_gt_free], inc="DV")
                P.ms_scb_free = red
                l1 = P.op("scalar", lambda e: e.activation(
                    z.GT[:, 8:16], z.GT[:, 0:8], AF.Ln, bias=z.cst1("zero")),
                    waits=[red], inc="AC")
                l2 = P.op("scalar", lambda e: e.activation(
                    z.GT[:, 8:16], z.GT[:, 8:16], AF.Exp, scale=0.5),
                    waits=[l1], inc="AC")
                mx = P.op("vector", lambda e: e.tensor_scalar(
                    z.GT[:, 8:16], z.GT[:, 8:16], 1e-12, None, OP.max),
                    waits=[l2], inc="DV")
                rc = P.op("vector", lambda e: e.reciprocal(z.GT[:, 8:16], z.GT[:, 8:16]),
                          waits=[mx], inc="DV")
                ev = None
                for hh in range(8):
                    h = vh * 8 + hh
                    dst = z.vb[:, (kc_abs * HEADS + h) * 65:(kc_abs * HEADS + h) * 65 + 64]
                    ev = P.op("vector", lambda e, dst=dst, hh=hh, bank=bank:
                              e.tensor_scalar(dst, z.pb[bank][:, hh * 64:hh * 64 + 64],
                                              z.GT[:, 8 + hh:9 + hh], None, OP.mult),
                              waits=[rc], inc="DV")
                P.ms_gt_free = ev
                P.bank_free[bank] = ev
    wo_ms = emit_weight_load(P, z.wD, w["wout"], D, D, waits=[P.now("PE")] + list(P.pool_free))
    for qb in range(NQB):
        t0 = qb * QB
        yn_ms = emit_norm_qb(P, z, z.X, lambda fci: qsl(z.xnq, fci), t0, kind="ln")
        mms = []
        for h in range(HEADS):
            g, sl = h // 2, (h % 2) * 64
            mm = None
            for kc in range(FC):
                mm = P.op("tensor", lambda e, h=h, kc=kc, g=g, sl=sl:
                          e.matmul(z.pb[g][sl:sl + 16, :],
                                   z.wQ0[:, kc * 256 + h * 16: kc * 256 + h * 16 + 16],
                                   qsl(z.xnq, kc), start=(kc == 0), stop=(kc == FC - 1)),
                          waits=([wq_ms, yn_ms, P.bank_free[g]] if kc == 0 else ()),
                          inc=("PE" if kc == FC - 1 else None))
            mms.append(mm)
        qev = None
        for h in range(HEADS):
            g, sl = h // 2, (h % 2) * 64
            qev = P.op("vector", lambda e, g=g, sl=sl:
                       e.tensor_copy(qsl(z.xnq, g)[sl:sl + 16, :],
                                     z.pb[g][sl:sl + 16, :]),
                       waits=[mms[-1]], inc="DV")
            P.bank_free[g] = qev
        for g in range(FC):
            qev = l2norm_16(lambda g=g: qsl(z.xnq, g), qev)
        q_ready = P.now("DV")
        for h in range(HEADS):
            g, sl = h // 2, (h % 2) * 64
            for kc in range(16):
                sbank = kc % 2
                smm = P.op("tensor", lambda e, kc=kc, g=g, sl=sl, sbank=sbank:
                           e.matmul(z.pb[sbank][:, :],
                                    z.kb16[sl:sl + 16, g * S + kc * 128: g * S + kc * 128 + 128],
                                    qsl(z.xnq, g)[sl:sl + 16, :],
                                    start=True, stop=True),
                           waits=[q_ready, P.bank_free[sbank]], inc="PE")
                rl = P.op("scalar", lambda e, sbank=sbank:
                          e.activation(z.SCR, z.pb[sbank][:, :], AF.Relu),
                          waits=[smm, P.ms_scr_free], inc="AC")
                P.bank_free[sbank] = rl
                pt = z.ptb[:, (kc % 2) * QB:(kc % 2) * QB + QB]
                sqr = P.op("vector", lambda e, pt=pt:
                           e.tensor_tensor(pt, z.SCR, z.SCR, OP.mult),
                           waits=[rl, P.pt_free[kc % 2]], inc="DV")
                P.ms_scr_free = sqr
                av_waits = [sqr] + ([P.bank_free[2]] if kc == 0 else [])
                avm = P.op("tensor", lambda e, kc=kc, h=h, pt=pt:
                           e.matmul(z.pb[2][0:64, :],
                                    z.vb[:, (kc * HEADS + h) * 65:(kc * HEADS + h) * 65 + 64],
                                    pt, start=(kc == 0), stop=(kc == 15)),
                           waits=av_waits, inc="PE")
                P.pt_free[kc % 2] = avm
            ao_last = P.op("vector", lambda e, g=g, sl=sl:
                           e.tensor_copy(z.ao[sl:sl + 64, g * QB:(g + 1) * QB],
                                         z.pb[2][0:64, :]),
                           waits=[avm], inc="DV")
            P.bank_free[2] = ao_last
        tn_last = None
        for fci in range(FC):
            sq = P.op("scalar", lambda e, fci=fci:
                      e.activation(z.SCB[:, 0:QB], z.ao[:, fci * QB:(fci + 1) * QB], AF.Square),
                      waits=[ao_last, P.ms_scb_free], inc="AC")
            mm = P.op("tensor", lambda e: e.matmul(
                z.pb[6][0:2, :], z.seg64b, z.SCB[:, 0:QB], start=True, stop=True),
                waits=[sq, P.bank_free[6]], inc="PE")
            P.ms_scb_free = mm
            l1 = P.op("scalar", lambda e: e.activation(
                z.ROWS2[0:2, 512:512 + QB], z.pb[6][0:2, :], AF.Ln, bias=z.cst1("zero", rows=2)),
                waits=[mm, P.ms_rowtemp], inc="AC")
            P.bank_free[6] = l1
            n1 = P.op("scalar", lambda e: e.activation(
                z.ROWS2[0:2, 512:512 + QB], z.ROWS2[0:2, 512:512 + QB], AF.Exp, scale=0.5),
                waits=[l1], inc="AC")
            th = P.op("scalar", lambda e: e.activation(
                z.ROWS2[0:2, 0:QB], z.ROWS2[0:2, 512:512 + QB], AF.Tanh),
                waits=[n1], inc="AC")
            mx = P.op("vector", lambda e: e.tensor_scalar(
                z.ROWS2[0:2, 512:512 + QB], z.ROWS2[0:2, 512:512 + QB], 1e-12, None, OP.max),
                waits=[n1, th], inc="DV")
            rc = P.op("vector", lambda e: e.reciprocal(
                z.ROWS2[0:2, 512:512 + QB], z.ROWS2[0:2, 512:512 + QB]),
                waits=[mx], inc="DV")
            ff = P.op("vector", lambda e: e.tensor_tensor(
                z.ROWS2[0:2, 0:QB], z.ROWS2[0:2, 0:QB], z.ROWS2[0:2, 512:512 + QB], OP.mult),
                waits=[rc], inc="DV")
            P.ms_rowtemp = ff
            ex = P.op("tensor", lambda e: e.matmul(
                z.pb[6][:, :], z.ex2, z.ROWS2[0:2, 0:QB], start=True, stop=True),
                waits=[ff, P.bank_free[6]], inc="PE")
            tn_last = P.op("vector", lambda e, fci=fci: e.tensor_tensor(
                z.ao[:, fci * QB:(fci + 1) * QB], z.ao[:, fci * QB:(fci + 1) * QB],
                z.pb[6][:, :], OP.mult),
                waits=[ex], inc="DV")
            P.bank_free[6] = tn_last
        for fci in range(FC):
            bank = 4 + fci % 2
            waits = [wo_ms, tn_last, P.bank_free[bank]]
            mm = None
            for c in range(FC):
                mm = P.op("tensor", lambda e, c=c, fci=fci, bank=bank:
                          e.matmul(z.pb[bank][:, :],
                                   z.wD[:, c * D + fci * 128: c * D + fci * 128 + 128],
                                   z.ao[:, c * QB:(c + 1) * QB],
                                   start=(c == 0), stop=(c == FC - 1)),
                          waits=(waits if c == 0 else ()), inc=("PE" if c == FC - 1 else None))
            ev = P.op("vector", lambda e, fci=fci, qb=qb, bank=bank:
                      e.tensor_tensor(xsl(z.X, fci, qb * QB, QB),
                                      z.pb[bank][:, :], xsl(z.X, fci, qb * QB, QB), OP.add),
                      waits=[mm], inc="DV")
            P.bank_free[bank] = ev


def emit_f0(P, z, w):
    emit_ln_rows(P, z, z.X, rms=False, eps="eps_ln")
    rw = [P.now("DV"), P.now("AC")]
    h_ms = None
    for qb in range(NQB):
        h_ms = emit_norm_qb(P, z, z.X, lambda fci, qb=qb: xsl(z.H, fci, qb * QB, QB),
                            qb * QB, kind="ln", row_waits=rw)
    park_ms = None
    pg = P.group("gpsimd")
    for fci in range(FC):
        park_ms = P.op("gpsimd", lambda e, fci=fci:
                       e.dma_start(out=P.xpark[fci * 128:(fci + 1) * 128, :],
                                   in_=xsl(z.X, fci, 0, S)),
                       waits=[h_ms], inc=pg, inc_amt=16)
    prev_free = P.pool_free
    P.pool_free = [park_ms] + list(prev_free)
    P.pool_free_g = [park_ms]
    za = carve_attn(z, "mla", "x")
    emit_ln_rows(P, z, z.H, rms=True, eps="eps_rms")
    emit_mla(P, za, w)
    mla_done = [P.now("DV"), P.now("AC")]
    if DBG["mode"] != "mla":
        zm = carve_moe(z, "x")
        P.pool_free = mla_done
        P.pool_free_g = mla_done
        P.hid_guard = None
        emit_ln_rows(P, z, z.H, rms=True, eps="eps_rms")
        emit_moe(P, zm, w, variant="l0", target=z.H)
    moe_done = [P.now("DV"), P.now("AC"), P.now("PE")]
    if DBG["mode"] == "mla-stop":
        return None
    rest_ms = None
    rg = P.group()
    for fci in range(FC):
        rest_ms = P.op("sync", lambda e, fci=fci:
                       e.dma_start(out=xsl(z.X, fci, 0, S),
                                   in_=P.xpark[fci * 128:(fci + 1) * 128, :]),
                       waits=moe_done + [park_ms], inc=rg, inc_amt=16)
    last = None
    for fci in range(FC):
        last = P.op("vector", lambda e, fci=fci:
                    e.scalar_tensor_tensor(xsl(z.X, fci, 0, S), xsl(z.H, fci, 0, S),
                                           0.5, xsl(z.X, fci, 0, S), OP.mult, OP.add),
                    waits=[rest_ms] + moe_done, inc="DV")
    return last


def build_phase(kind):
    nc = bass.Bass("TRN2", target_bir_lowering=False)
    x_dram = nc.dram_tensor("x", [NSEQ, D, S], F32, kind="ExternalInput").ap()
    y_dram = nc.dram_tensor("y", [NSEQ, D, S], F32, kind="ExternalOutput").ap()
    cf_dram = nc.dram_tensor("cf", [128, CF_COLS], F32, kind="ExternalInput").ap()
    cb_dram = nc.dram_tensor("cb", [128, CB_COLS], BF16, kind="ExternalInput").ap()
    wd = {}

    def win(name, r, c):
        wd[name] = nc.dram_tensor(name, [r, c], BF16, kind="ExternalInput").ap()
        return wd[name]

    if kind == "f0":
        w = dict(wq=win("cv_l0f_wq", D, D), wdkv=win("cv_l0f_wdkv", D, LAT),
                 wuk=win("cv_l0f_wuk", LAT, D), wuv=win("cv_l0f_wuv", LAT, D),
                 wo=win("cv_l0f_wo", D, D), gate=win("cv_l0f_gate", D, 8),
                 w1=win("cv_l0f_w1", EXP * D, EDIM), w2=win("cv_l0f_w2", EXP * EDIM, D))
    elif kind == "g1":
        w = dict(gate=win("cv_l1f_wg", D, 8),
                 w1=win("cv_l1f_w1", EXP * D, EDIM), w2=win("cv_l1f_w2", EXP * EDIM, D))
    elif kind == "a0":
        w = dict(wq=win("cv_l0a_wq", D, 256), wk=win("cv_l0a_wk", D, 256),
                 wv=win("cv_l0a_wv", D, D), wout=win("cv_l0a_wout", D, D))
    elif kind == "a1":
        w = dict(wqT=win("cv_l1a_wqT", 64, 64), wkT=win("cv_l1a_wkT", 64, 64),
                 wvT=win("cv_l1a_wvT", 64, 64), woT=win("cv_l1a_woT", D, D))

    aux = {}
    if kind in ("f0", "g1"):
        aux["sel8f"] = nc.dram_tensor("sel8f", [8, 1024], BF16, kind="ExternalInput").ap()
    if kind in ("f0", "a1"):
        aux["cmaskb"] = nc.dram_tensor("cmaskb", [128, 2048], BF16, kind="ExternalInput").ap()
    if kind == "f0":
        aux["g1c"] = nc.dram_tensor("g1c", [128, 8], F32, kind="ExternalInput").ap()
        aux["g2c"] = nc.dram_tensor("g2c", [128, 8], F32, kind="ExternalInput").ap()
    if kind == "g1":
        aux["b1f"] = nc.dram_tensor("b1f", [128, 128], F32, kind="ExternalInput").ap()
        aux["b2f"] = nc.dram_tensor("b2f", [128, 64], F32, kind="ExternalInput").ap()

    with ExitStack() as ctx:
        z = carve(nc, ctx)
        P = Prog(nc, ctx)
        if kind == "f0":
            P.xpark = nc.dram_tensor("xpark", [D, S], F32, kind="Internal").ap()
        if kind in ("f0", "g1"):
            P.sel8_dram = aux["sel8f"]
        if kind in ("f0", "a1"):
            P.cmask_dram = aux["cmaskb"]
        block = ctx.enter_context(nc.Block())
        cg = P.group()
        cms = [P.op("sync", lambda e: e.dma_start(out=z.CSTF, in_=cf_dram), inc=cg, inc_amt=16),
               P.op("sync", lambda e: e.dma_start(out=z.CSTB, in_=cb_dram), inc=cg, inc_amt=16)]
        if kind == "f0":
            cms.append(P.op("sync", lambda e: e.dma_start(out=z.g1c, in_=aux["g1c"]),
                            inc=cg, inc_amt=16))
            cms.append(P.op("sync", lambda e: e.dma_start(out=z.g2c, in_=aux["g2c"]),
                            inc=cg, inc_amt=16))
        if kind == "g1":
            cms.append(P.op("sync", lambda e: e.dma_start(out=z.b1f, in_=aux["b1f"]),
                            inc=cg, inc_amt=16))
            cms.append(P.op("sync", lambda e: e.dma_start(out=z.b2f, in_=aux["b2f"]),
                            inc=cg, inc_amt=16))
        x_free = None
        st_groups = []
        for s in range(DBG["nseq"] or NSEQ):
            P.pool_free = [P.now("DV"), P.now("AC")] if s > 0 else [cms[-1]]
            P.pool_free_g = [x_free] if x_free else []
            ld = None
            lg = P.group()
            for fci in range(FC):
                ld = P.op("sync", lambda e, fci=fci, s=s:
                          e.dma_start(out=xsl(z.X, fci, 0, S),
                                      in_=x_dram[s, fci * 128:(fci + 1) * 128, :]),
                          waits=[x_free], inc=lg, inc_amt=16)
            P.op("vector", lambda e: e.nop(), waits=[ld] + cms, inc="DV")
            if kind == "f0":
                emit_f0(P, z, w)
            elif kind == "g1":
                zm = carve_moe(z, "h")
                P.hid_guard = None
                if DBG["mode"] == "id":
                    pass
                elif DBG["mode"] == "ln":
                    emit_ln_rows(P, z, z.X, rms=False, eps="eps_ln")
                    rw = [P.now("DV"), P.now("AC")]
                    for qb in range(NQB):
                        emit_norm_qb(P, z, z.X,
                                     lambda fci, qb=qb: xsl(z.X, fci, qb * QB, QB),
                                     qb * QB, kind="ln", row_waits=rw)
                else:
                    emit_ln_rows(P, z, z.X, rms=False, eps="eps_ln")
                    emit_moe(P, zm, w, variant="l1", target=z.X)
            elif kind == "a0":
                za = carve_attn(z, "a0", "h")
                emit_attn0(P, za, w, first_seq=(s == 0))
            elif kind == "a1":
                za = carve_attn(z, "a1", "h")
                emit_attn1(P, za, w)
            allms = [P.now("DV"), P.now("AC"), P.now("PE")]
            sg = P.group("gpsimd")
            for fci in range(FC):
                x_free = P.op("gpsimd", lambda e, fci=fci, s=s:
                              e.dma_start(out=y_dram[s, fci * 128:(fci + 1) * 128, :],
                                          in_=xsl(z.X, fci, 0, S)),
                              waits=allms, inc=sg, inc_amt=16)
            st_groups.append(x_free)
        P.op("gpsimd", lambda e: e.nop(), waits=st_groups)
        P.play(block)
    return nc


_cache = {}


def _get(name, builder):
    if name not in _cache:
        _cache[name] = builder()
    return _cache[name]


def _conv_inputs(inputs):
    tr = np.ascontiguousarray
    kcols = (np.arange(HEADS)[:, None] * 80 + np.arange(16)[None, :]).reshape(-1)
    vcols = (np.arange(HEADS)[:, None] * 80 + 16 + np.arange(64)[None, :]).reshape(-1)
    wkv = np.asarray(inputs["L0_attn_Wkv"], np.float32)
    d = {
        "l0a_wq": inputs["L0_attn_Wq"],
        "l0a_wk": tr(wkv[:, kcols]),
        "l0a_wv": tr(wkv[:, vcols]),
        "l0a_wout": inputs["L0_attn_Wout"],
        "l0f_wq": inputs["L0_ffn_Wq"],
        "l0f_wdkv": inputs["L0_ffn_Wdkv"],
        "l0f_wuk": inputs["L0_ffn_Wuk"],
        "l0f_wuv": inputs["L0_ffn_Wuv"],
        "l0f_wo": inputs["L0_ffn_Wo"],
        "l0f_gate": inputs["L0_ffn_gate_w"],
        "l0f_w1": np.asarray(inputs["L0_ffn_w1"]).reshape(EXP * D, EDIM),
        "l0f_w2": np.asarray(inputs["L0_ffn_w2"]).reshape(EXP * EDIM, D),
        "l1a_wqT": tr(np.asarray(inputs["L1_attn_Wq"]).T),
        "l1a_wkT": tr(np.asarray(inputs["L1_attn_Wk"]).T),
        "l1a_wvT": tr(np.asarray(inputs["L1_attn_Wv"]).T),
        "l1a_woT": tr(np.asarray(inputs["L1_attn_Wo"]).T),
        "l1f_wg": inputs["L1_ffn_Wg"],
        "l1f_w1": np.asarray(inputs["L1_ffn_W1"]).reshape(EXP * D, EDIM),
        "l1f_w2": np.asarray(inputs["L1_ffn_W2"]).reshape(EXP * EDIM, D),
    }
    return {k: np.ascontiguousarray(v, dtype=np.float32) for k, v in d.items()}


def kernel(**inputs):
    cin = _conv_inputs(inputs)
    nc_conv = _get("conv", build_conv)
    conv_in = {name: cin[name] for name, _, _ in CONV_SPECS}
    res = run_bass_kernel_spmd(nc_conv, [conv_in] * NCORES, core_ids=list(range(NCORES)))
    cw = dict(res.results[0])

    x = np.asarray(inputs["L0_attn_tokens"], dtype=np.float32)
    xc_ = x.reshape(NCORES, NSEQ, S, D).transpose(0, 1, 3, 2)
    xs = [np.ascontiguousarray(xc_[i]) for i in range(NCORES)]
    g1c = np.ascontiguousarray(np.asarray(inputs["L0_ffn_g1"], np.float32).reshape(FC, 128).T)
    g2c = np.ascontiguousarray(np.asarray(inputs["L0_ffn_g2"], np.float32).reshape(FC, 128).T)
    b1 = np.asarray(inputs["L1_ffn_b1"], np.float32)
    b2 = np.asarray(inputs["L1_ffn_b2"], np.float32)
    b1f = np.ascontiguousarray(b1.reshape(EXP, 16, 128).transpose(2, 0, 1).reshape(128, 128))
    b2f = np.ascontiguousarray(b2.reshape(EXP, 8, 128).transpose(2, 0, 1).reshape(128, 64))

    def run(kind, xs_cur, extra):
        nc = _get(kind, lambda: build_phase(kind))
        maps = []
        for i in range(NCORES):
            m = {"x": xs_cur[i], "cf": CF, "cb": CB}
            m.update(extra)
            maps.append(m)
        r = run_bass_kernel_spmd(nc, maps, core_ids=list(range(NCORES)))
        return [r.results[i]["y"] for i in range(NCORES)]

    f0_w = {k: cw[k] for k in ("cv_l0f_wq", "cv_l0f_wdkv", "cv_l0f_wuk", "cv_l0f_wuv",
                               "cv_l0f_wo", "cv_l0f_gate", "cv_l0f_w1", "cv_l0f_w2")}
    f0_w.update({"g1c": g1c, "g2c": g2c, "sel8f": SEL8H, "cmaskb": CMASKH})
    a0_w = {k: cw[k] for k in ("cv_l0a_wq", "cv_l0a_wk", "cv_l0a_wv", "cv_l0a_wout")}
    g1_w = {k: cw[k] for k in ("cv_l1f_wg", "cv_l1f_w1", "cv_l1f_w2")}
    g1_w.update({"b1f": b1f, "b2f": b2f, "sel8f": SEL8H})
    a1_w = {k: cw[k] for k in ("cv_l1a_wqT", "cv_l1a_wkT", "cv_l1a_wvT", "cv_l1a_woT")}
    a1_w["cmaskb"] = CMASKH

    xs = run("f0", xs, f0_w)
    xs = run("a0", xs, a0_w)
    xs = run("f0", xs, f0_w)
    xs = run("g1", xs, g1_w)
    xs = run("a1", xs, a1_w)
    xs = run("g1", xs, g1_w)

    out = np.stack(xs).reshape(NCORES, NSEQ, D, S).transpose(0, 1, 3, 2)
    return np.ascontiguousarray(out.reshape(B, S, D), dtype=np.float32)


def build_fused():
    nc = bass.Bass("TRN2", target_bir_lowering=False)
    x_dram = nc.dram_tensor("x", [NSEQ, D, S], F32, kind="ExternalInput").ap()
    y_dram = nc.dram_tensor("y", [NSEQ, D, S], F32, kind="ExternalOutput").ap()
    cf_dram = nc.dram_tensor("cf", [128, CF_COLS], F32, kind="ExternalInput").ap()
    cb_dram = nc.dram_tensor("cb", [128, CB_COLS], BF16, kind="ExternalInput").ap()
    sel_dram = nc.dram_tensor("sel8f", [8, 1024], BF16, kind="ExternalInput").ap()
    cm_dram = nc.dram_tensor("cmaskb", [128, 2048], BF16, kind="ExternalInput").ap()
    aux = {}
    for name, shape in (("g1c", [128, 8]), ("g2c", [128, 8]),
                        ("b1f", [128, 128]), ("b2f", [128, 64])):
        aux[name] = nc.dram_tensor(name, shape, F32, kind="ExternalInput").ap()
    wsrc, wdst = {}, {}
    for name, r, c in CONV_SPECS:
        wsrc[name] = nc.dram_tensor(name, [r, c], F32, kind="ExternalInput").ap()
        wdst[name] = nc.dram_tensor("cv_" + name, [r, c], BF16, kind="Internal").ap()

    with ExitStack() as ctx:
        z = carve(nc, ctx)
        P = Prog(nc, ctx)
        P.xpark = nc.dram_tensor("xpark", [D, S], F32, kind="Internal").ap()
        P.sel8_dram = sel_dram
        P.cmask_dram = cm_dram
        block = ctx.enter_context(nc.Block())
        cg = P.group()
        cms = [P.op("sync", lambda e: e.dma_start(out=z.CSTF, in_=cf_dram), inc=cg, inc_amt=16),
               P.op("sync", lambda e: e.dma_start(out=z.CSTB, in_=cb_dram), inc=cg, inc_amt=16)]
        for name in ("g1c", "g2c", "b1f", "b2f"):
            cms.append(P.op("sync", lambda e, name=name: e.dma_start(
                out=getattr(z, name), in_=aux[name]), inc=cg, inc_amt=16))
        cvp = pool(z, "h")
        stg = [cvp(4096), cvp(4096)]
        ob = [cvp(2048, BF16), cvp(2048, BF16)]
        free_ms = [None, None]
        st_ms = [None, None]
        conv_stores = []
        slot = 0
        for name, r, c in CONV_SPECS:
            nchunk = max(1, r // 128)
            rows = min(r, 128)
            for i in range(nchunk):
                for j0 in range(0, c, 4096):
                    w = min(4096, c - j0)
                    sl = slot % 2
                    slot += 1
                    if r >= 128:
                        src = wsrc[name].rearrange("(n p) c -> n p c", p=128)[i, :, j0:j0 + w]
                        dst = wdst[name].rearrange("(n p) c -> n p c", p=128)[i, :, j0:j0 + w]
                    else:
                        src = wsrc[name][:, j0:j0 + w]
                        dst = wdst[name][:, j0:j0 + w]
                    ld = P.op("sync", lambda e, s=src, t=stg[sl], rr=rows, ww=w:
                              e.dma_start(out=t[0:rr, 0:ww], in_=s),
                              waits=[free_ms[sl]], inc=P.group(), inc_amt=16)
                    cv = P.op("vector", lambda e, t=stg[sl], o=ob[sl], rr=rows, ww=w:
                              e.tensor_copy(o[0:rr, 0:ww], t[0:rr, 0:ww]),
                              waits=[ld, st_ms[sl]], inc="DV")
                    free_ms[sl] = cv
                    st = P.op("sync", lambda e, o=ob[sl], d=dst, rr=rows, ww=w:
                              e.dma_start(out=d, in_=o[0:rr, 0:ww]),
                              waits=[cv], inc=P.group(), inc_amt=16)
                    st_ms[sl] = st
                    conv_stores.append(st)
        P.op("vector", lambda e: e.nop(), waits=conv_stores + cms, inc="DV")

        def wap(*names):
            return {k: wdst["cv_" + v] if False else wdst[v] for k, v in names}
        w_f0 = dict(wq=wdst["l0f_wq"], wdkv=wdst["l0f_wdkv"], wuk=wdst["l0f_wuk"],
                    wuv=wdst["l0f_wuv"], wo=wdst["l0f_wo"], gate=wdst["l0f_gate"],
                    w1=wdst["l0f_w1"], w2=wdst["l0f_w2"])
        w_a0 = dict(wq=wdst["l0a_wq"], wk=wdst["l0a_wk"], wv=wdst["l0a_wv"],
                    wout=wdst["l0a_wout"])
        w_g1 = dict(gate=wdst["l1f_wg"], w1=wdst["l1f_w1"], w2=wdst["l1f_w2"])
        w_a1 = dict(wqT=wdst["l1a_wqT"], wkT=wdst["l1a_wkT"], wvT=wdst["l1a_wvT"],
                    woT=wdst["l1a_woT"])

        x_free = None
        st_groups = []

        def fresh_pool():
            P.pool_free = [P.now("DV"), P.now("AC")]
            P.pool_free_g = [P.now("DV"), P.now("AC")]

        for s in range(DBG["nseq"] or NSEQ):
            lg = P.group()
            ld = None
            for fci in range(FC):
                ld = P.op("sync", lambda e, fci=fci, s=s:
                          e.dma_start(out=xsl(z.X, fci, 0, S),
                                      in_=x_dram[s, fci * 128:(fci + 1) * 128, :]),
                          waits=[x_free], inc=lg, inc_amt=16)
            P.op("vector", lambda e: e.nop(), waits=[ld], inc="DV")
            fresh_pool()
            emit_f0(P, z, w_f0)
            fresh_pool()
            za0 = carve_attn(z, "a0", "h")
            emit_attn0(P, za0, w_a0, first_seq=(s == 0))
            fresh_pool()
            emit_f0(P, z, w_f0)
            fresh_pool()
            zm1 = carve_moe(z, "h")
            P.hid_guard = None
            emit_ln_rows(P, z, z.X, rms=False, eps="eps_ln")
            emit_moe(P, zm1, w_g1, variant="l1", target=z.X)
            fresh_pool()
            za1 = carve_attn(z, "a1", "h")
            emit_attn1(P, za1, w_a1)
            fresh_pool()
            zm2 = carve_moe(z, "h")
            P.hid_guard = None
            emit_ln_rows(P, z, z.X, rms=False, eps="eps_ln")
            emit_moe(P, zm2, w_g1, variant="l1", target=z.X)
            allms = [P.now("DV"), P.now("AC"), P.now("PE")]
            sg = P.group("gpsimd")
            for fci in range(FC):
                x_free = P.op("gpsimd", lambda e, fci=fci, s=s:
                              e.dma_start(out=y_dram[s, fci * 128:(fci + 1) * 128, :],
                                          in_=xsl(z.X, fci, 0, S)),
                              waits=allms, inc=sg, inc_amt=16)
            st_groups.append(x_free)
        P.op("gpsimd", lambda e: e.nop(), waits=st_groups)
        P.play(block)
    return nc


def kernel_fused(**inputs):
    cin = _conv_inputs(inputs)
    nc = _get("fused", build_fused)
    x = np.asarray(inputs["L0_attn_tokens"], dtype=np.float32)
    xc_ = x.reshape(NCORES, NSEQ, S, D).transpose(0, 1, 3, 2)
    g1c = np.ascontiguousarray(np.asarray(inputs["L0_ffn_g1"], np.float32).reshape(FC, 128).T)
    g2c = np.ascontiguousarray(np.asarray(inputs["L0_ffn_g2"], np.float32).reshape(FC, 128).T)
    b1 = np.asarray(inputs["L1_ffn_b1"], np.float32)
    b2 = np.asarray(inputs["L1_ffn_b2"], np.float32)
    b1f = np.ascontiguousarray(b1.reshape(EXP, 16, 128).transpose(2, 0, 1).reshape(128, 128))
    b2f = np.ascontiguousarray(b2.reshape(EXP, 8, 128).transpose(2, 0, 1).reshape(128, 64))
    maps = []
    for i in range(NCORES):
        m = {"x": np.ascontiguousarray(xc_[i]), "cf": CF, "cb": CB, "sel8f": SEL8H,
             "cmaskb": CMASKH, "g1c": g1c, "g2c": g2c, "b1f": b1f, "b2f": b2f}
        m.update(cin)
        maps.append(m)
    r = run_bass_kernel_spmd(nc, maps, core_ids=list(range(NCORES)))
    out = np.stack([r.results[i]["y"] for i in range(NCORES)])
    out = out.reshape(NCORES, NSEQ, D, S).transpose(0, 1, 3, 2)
    return np.ascontiguousarray(out.reshape(B, S, D), dtype=np.float32)


USE_FUSED = True
_kernel_unfused = kernel


def kernel(**inputs):
    if USE_FUSED:
        return kernel_fused(**inputs)
    return _kernel_unfused(**inputs)
